# version 06
# baseline (speedup 1.0000x reference)
import contextlib
import numpy as np
import ml_dtypes

import concourse.bass as bass
import concourse.mybir as mybir
from concourse.bass_utils import run_bass_kernel_spmd

F32 = mybir.dt.float32
F16 = mybir.dt.float16
BF16 = mybir.dt.bfloat16
I8 = mybir.dt.int8
AF = mybir.ActivationFunctionType
ALU = mybir.AluOpType
AX = mybir.AxisListType

P = 128
NCORES = 8
BLOC = 2
S = 2048
NT = S // P
D = 1024
ND = D // P
F = 3072
NF = F // P
F1 = 4096
NF1 = F1 // P
NSG = 4
SB_TOP = 229376 - 128

MAGIC = 1536.0


class Tok(dict):
    def merged(self, other):
        t = Tok(self)
        for k, v in (other or {}).items():
            t[k] = max(t.get(k, 0), v)
        return t


def row_busy_get(rb, r):
    return rb[0] if r in (0, 64) else rb[1]


def row_busy_set(rb, r, tok):
    if r in (0, 64):
        rb[0] = tok
    else:
        rb[1] = tok


def merge(*toks):
    t = Tok()
    for o in toks:
        if o:
            t = t.merged(o)
    return t


class Prog:
    ENGINES = ("tensor", "vector", "scalar", "gpsimd", "sync")

    def __init__(self, nc, es):
        self.nc = nc
        self.es = es
        self.streams = {k: [] for k in self.ENGINES}
        self.sems = {}
        self.guaranteed = {k: {} for k in self.ENGINES}

    def sem(self, name):
        h = self.es.enter_context(self.nc.semaphore(name))
        self.sems[name] = [h, 0]

    def dsem(self, name):
        if name not in self.sems:
            self.sem(name)
        return name

    OWN = {"vector": "p_dve", "scalar": "p_act", "gpsimd": "p_gp"}

    def emit(self, eng, fn, wait=None, inc=None, inc_amt=1):
        rec = None
        if inc is not None:
            rec = self.sems[inc]
            rec[1] += inc_amt
        wait = dict(wait or {})
        own = self.OWN.get(eng)
        if own is not None:
            prior = self.sems[own][1] - (inc_amt if inc == own else 0)
            if prior > 0:
                wait[own] = max(wait.get(own, 0), prior)
        waits = []
        g = self.guaranteed[eng]
        for name, v in wait.items():
            if g.get(name, -1) < v:
                waits.append((self.sems[name][0], v))
                g[name] = v
        sem_h = rec[0] if rec is not None else None

        def run(e, _fn=fn, _w=waits, _sh=sem_h, _amt=inc_amt):
            for hh, vv in _w:
                e.wait_ge(hh, vv)
            ins = _fn(e)
            if _sh is not None:
                ins.then_inc(_sh, _amt)
            return ins

        self.streams[eng].append(run)
        return Tok({inc: rec[1]}) if inc is not None else None

    def fence_all(self):
        return Tok({k: v[1] for k, v in self.sems.items() if v[1] > 0})

    def run(self, block):
        for name in self.ENGINES:
            stream = self.streams[name]

            def body(e, _s=stream):
                for fn in _s:
                    fn(e)

            getattr(block, name)(body)


def host_constants():
    c = {}
    c["ident_f32"] = np.eye(P, dtype=np.float32)
    c["ident_bf16"] = np.eye(P, dtype=np.float32).astype(ml_dtypes.bfloat16)
    c["ones_row"] = np.ones((1, P), dtype=np.float32)
    pos = np.arange(S, dtype=np.float32)
    inv_freq = (1.0 / 10000.0 ** (np.arange(0, 32, 2, dtype=np.float32) / 32.0)
                ).astype(np.float32)
    sinusoid = np.einsum("i,j->ij", pos, inv_freq).astype(np.float32)
    def tilemaj(t):
        return np.ascontiguousarray(
            t.reshape(NT, P, 16).transpose(1, 0, 2).reshape(P, NT * 16))
    c["sin_t"] = tilemaj(np.sin(sinusoid).astype(np.float32))
    c["cos_t"] = tilemaj(np.cos(sinusoid).astype(np.float32))
    i = np.arange(P)[:, None]
    j = np.arange(512)[None, :]
    masks = [np.where(j <= i + 128 * r, 0.0, -3e38).astype(np.float32)
             for r in range(4)]
    c["masks"] = np.concatenate(masks, axis=1).astype(ml_dtypes.bfloat16)
    return c


class MK:
    def __init__(self, debug=(), limit=99, slimit=99, climit=99, wlimit=99,
                 qlimit=99):
        self.debug = set(debug)
        self.limit = limit
        self.slimit = slimit
        self.climit = climit
        self.wlimit = wlimit
        self.qlimit = qlimit
        self.nc = bass.Bass("TRN2", target_bir_lowering=False)
        self.dbg = {}

    def declare(self):
        nc = self.nc
        self.x_d = nc.dram_tensor("x", [BLOC, S, D], F32, kind="ExternalInput").ap()
        self.wqkv_d = nc.dram_tensor("wqkv", [F, D], F32, kind="ExternalInput").ap()
        self.wout_d = nc.dram_tensor("wout", [D, D], F32, kind="ExternalInput").ap()
        self.w1_d = nc.dram_tensor("w1", [F1, D], F32, kind="ExternalInput").ap()
        self.convw_d = nc.dram_tensor("convw", [P, NF1 * 3], F32,
                                      kind="ExternalInput").ap()
        self.convb_d = nc.dram_tensor("convb", [P, NF1], F32,
                                      kind="ExternalInput").ap()
        self.w2_d = nc.dram_tensor("w2", [D, F1], F32, kind="ExternalInput").ap()
        self.out_d = nc.dram_tensor("out", [BLOC, S, D], F32, kind="ExternalOutput").ap()

        self.s_wqkvT = {p: nc.dram_tensor(f"s_wqkvT_{p}", [ND, NF, P, P], BF16).ap()
                        for p in ("hi", "lo")}
        self.s_woutT = {p: nc.dram_tensor(f"s_woutT_{p}", [ND, ND, P, P], BF16).ap()
                        for p in ("hi", "lo")}
        self.s_w1qT = nc.dram_tensor("s_w1qT", [ND, NF1, P, P], I8).ap()
        self.s_w2qT = nc.dram_tensor("s_w2qT", [NF1, ND, P, P], I8).ap()
        self.s_xhat = {p: nc.dram_tensor(f"s_xhat_{p}", [BLOC, ND, NT, P, P], BF16).ap()
                       for p in ("hi", "lo")}

        consts = host_constants()
        self.inl = {k: nc.inline_tensor(v, name=f"c_{k}") for k, v in consts.items()}

        A = lambda name, shape, dtype: nc.alloc_sbuf_tensor(name, shape, dtype).ap()
        self.ident_f = A("ident_f", [P, P], F32)
        self.ident_b = A("ident_b", [P, P], BF16)
        self.ones_r = A("ones_r", [1, P], F32)
        self.cos_s = A("cos_s", [P, NT * 16], F32)
        self.sin_s = A("sin_s", [P, NT * 16], F32)
        self.mask_s = A("mask_s", [P, 2048], BF16)
        self.convw_s = A("convw_s", [P, NF1 * 3], F32)
        self.convb_s = A("convb_s", [P, NF1], F32)
        self.wsc = A("wsc", [P, 8], F32)
        self.ln_m = A("ln_m", [P, NT], F32)
        self.ln_r = A("ln_r", [P, NT], F32)
        self.ln2_m = A("ln2_m", [P, NT], F32)
        self.ln2_r = A("ln2_r", [P, NT], F32)
        self.mx_acc = A("mx_acc", [P, 8], F32)
        self.m0_bc = A("m0_bc", [P, 4], F32)
        self.qn2 = A("qn2", [P, NT * 4], F32)
        self.mbias = A("mbias", [P, NT * 4], F32)
        self.sc_a = A("sc_a", [P, 16], F32)
        self.eps_c = A("eps_c", [P, 1], F32)
        self.sc_b = A("sc_b", [P, 16], F32)
        self.qrow = A("qrow", [P, P], F32)
        self.sums_c = A("sums_c", [P, 16], F32)
        self.recip_c = A("recip_c", [P, 4], F32)
        self.s1tok = A("s1tok", [P, NT], F32)
        self.rs1tok = A("rs1tok", [P, NT], F32)
        self.s2tok = A("s2tok", [P, 8], F32)
        self.rs2tok = A("rs2tok", [P, 8], F32)
        self.prev2 = A("prev2", [P, NF1 * 2], F32)

        off = [SB_TOP]

        def arena(name, shape, dtype, align=128):
            nbytes = int(np.prod(shape[1:])) * mybir.dt.size(dtype)
            nbytes = (nbytes + align - 1) // align * align
            off[0] -= nbytes
            return nc.alloc_sbuf_tensor_at(name, shape, dtype, offset=off[0]).ap()

        def at(name, shape, dtype, o):
            return nc.alloc_sbuf_tensor_at(name, shape, dtype, offset=o).ap()

        self.x1_t = arena("x1_t", [P, NT, D], F32)
        self.h_t = arena("h_t", [P, NF1, 516], F32)
        hbase = off[0]
        self.heads_t = at("heads_t", [P, NT, D], F32, hbase)
        self.su_w = at("su_w", [P, F1], F32, hbase)
        self.su_q = at("su_q", [P, F1], F16, hbase + F1 * 4)
        self.su_b = at("su_b", [P, F1], BF16, hbase + F1 * 4 + F1 * 2)
        self.su_e = at("su_e", [P, F1], F32, hbase + F1 * 8)
        self.wq_stage = arena("wq_stage", [P, 2, 2, 768], BF16)
        self.kq_rot = arena("kq_rot", [P, 2, 512], F32)
        self.rope_tmp = arena("rope_tmp", [P, 2, 128], F32)
        self.xst_ring = arena("xst_ring", [P, 8, P], BF16)
        self.v_tok = arena("v_tok", [P, NT, 256], F32)
        self.kqf = arena("kqf", [P, 4, S], F32)
        self.attn_sb = arena("attn_sb", [P, 2, 512], F32)
        self.attnT_sb = arena("attnT_sb", [P, 2, 512], F32)
        wlow = off[0]
        self.wlow = wlow
        o = wlow
        self.ln_x = at("ln_x", [P, 2, D], F32, o); o += 2 * D * 4
        self.ln_xh = at("ln_xh", [P, 2, D], F32, o); o += 2 * D * 4
        self.ln_hi = at("ln_hi", [P, 2, D], BF16, o); o += 2 * D * 2
        self.ln_lo = at("ln_lo", [P, 2, D], BF16, o); o += 2 * D * 2
        self.ln_st = at("ln_st", [P, 2, 1024], BF16, o); o += 2 * 1024 * 2
        self.su_ev = at("su_ev", [P, 2, 1024], BF16, wlow)
        o = wlow
        self.hfw = at("hfw", [P, 2, ND, 512], BF16, o); o += 2 * ND * 512 * 2
        self.wo_stage = at("wo_stage", [P, 6, 2, P], BF16, o); o += 6 * 2 * P * 2
        self.ao_ev = at("ao_ev", [P, 2, 512], F32, o); o += 2 * 512 * 4
        self.xres = at("xres", [P, 4, D], F32, o); o += 4 * D * 4
        o = wlow
        self.hq_t = at("hq_t", [P, NF1, 514], BF16, o); o += NF1 * 514 * 2
        self.xq_win = at("xq_win", [P, ND, 512], BF16, o); o += ND * 512 * 2
        self.x1h_ring = at("x1h_ring", [P, 1, D], F32, o); o += 1 * D * 4
        self.w12_ring = at("w12_ring", [P, 4, 8, P], BF16, o); o += 4 * 8 * P * 2
        self.xqb_st = at("xqb_st", [P, D], BF16, o); o += D * 2
        self.yw = at("yw", [P, 512], F32, o); o += 512 * 4
        self.bc_q = at("bc_q", [P, 544], F32, o); o += 544 * 4
        self.bc_r = at("bc_r", [P, 544], F32, o); o += 544 * 4
        self.acc_c = at("acc_c", [P, 544], F32, o); o += 544 * 4
        self.acc2 = at("acc2", [P, 544], F32, o); o += 544 * 4
        self.qf16 = at("qf16", [P, 1024], F16, o); o += 1024 * 2
        self.bc_w1 = self.bc_q
        ffn_chain_top = o

        self.arena_low = off[0]
        assert ffn_chain_top <= hbase, (ffn_chain_top, hbase)
        used = 212863 - nc.sbuf_bytes_remaining
        statics_top = 16512 + used
        assert statics_top <= off[0], (statics_top, off[0])

        self.PB = [nc.alloc_psum_tensor(f"pb{i}", [P, 512], F32).ap()
                   for i in range(7)]
        self.PBB = nc.alloc_psum_tensor("pbb", [P, 1024], BF16).ap()

    def dbg_out(self, name, shape, dtype=F32):
        if name in self.debug:
            ap = self.nc.dram_tensor(f"dbg_{name}", shape, dtype,
                                     kind="ExternalOutput").ap()
            self.dbg[name] = ap
            return ap
        return None

    def st(self, dst, src, wait=None, q="sync", dsem="d_sp", accum=False):
        sem = self.pr.dsem(dsem)
        prev = self.pr.sems[sem][1]
        wait = dict(wait or {})
        if prev > 0:
            wait[sem] = max(wait.get(sem, 0), prev)
        if accum:
            fn = lambda e, d=dst, s=src: e.dma_start(out=d, in_=s,
                                                     accum_op=ALU.add)
        else:
            fn = lambda e, d=dst, s=src: e.dma_start(out=d, in_=s)
        return self.pr.emit(q, fn, wait=wait, inc=sem, inc_amt=16)

    def ld(self, dst, src, wait=None, q="sync", dsem="d_sp"):
        return self.st(dst, src, wait=wait, q=q, dsem=dsem)

    def ev(self, eng, fn, wait=None):
        sem = {"tensor": "p_pe", "vector": "p_dve", "scalar": "p_act",
               "gpsimd": "p_gp"}[eng]
        return self.pr.emit(eng, fn, wait=wait, inc=sem)

    def emit_consts(self):
        ld = lambda d, s, **kw: self.ld(d, s, dsem="dc", **kw)
        toks = [self.ev("vector", lambda e: e.memset(self.eps_c[:], 1e-5))]
        loads = [
            (self.ident_f[:], self.inl["ident_f32"].ap()),
            (self.ident_b[:], self.inl["ident_bf16"].ap()),
            (self.ones_r[:], self.inl["ones_row"].ap()),
            (self.cos_s[:], self.inl["cos_t"].ap()),
            (self.sin_s[:], self.inl["sin_t"].ap()),
            (self.mask_s[:], self.inl["masks"].ap()),
            (self.convw_s[:], self.convw_d),
            (self.convb_s[:], self.convb_d),
        ]
        for i, (d, s_) in enumerate(loads):
            if i < self.climit:
                toks.append(ld(d, s_))
        self.tok_consts = merge(*toks)

    def emit_wprep_quant(self, w_d, n_tiles, tile_cols, s_col, dst, dst_f_outer):
        ev, ld = self.ev, self.ld
        NC = tile_cols // P
        tok_z = ev("vector", lambda e: e.memset(self.sc_a[:, 0:1], 0.0))
        tk = tok_z
        for i in range(n_tiles):
            tl = ld(self.su_w[:, 0:tile_cols], w_d[i * P:(i + 1) * P, :],
                    wait=merge(tk, self.su_w_dma), dsem="dsuw")
            self.su_w_dma = tl
            tk = ev("vector", lambda e: e.reduce_sum(
                self.sc_a[:, 1:2], self.su_w[:, 0:tile_cols], axis=AX.X,
                apply_absolute_value=True), wait=merge(tl, tok_z))
            tk = ev("vector", lambda e: e.tensor_add(
                self.sc_a[:, 0:1], self.sc_a[:, 0:1], self.sc_a[:, 1:2]))
        self.last_q3 = merge(self.last_q3, tk)
        tk_t = ev("tensor", lambda e: e.transpose(
            self.PB[0][0:1, 0:P], self.sc_a[:, 0:1], self.ident_f[:]),
            wait=merge(tk, self.tok_consts, self.pb_free[0]))
        tk = ev("vector", lambda e: e.reduce_sum(
            self.sc_b[0:1, 0:1], self.PB[0][0:1, 0:P], axis=AX.X), wait=tk_t)
        n_el = n_tiles * P * tile_cols
        tk = ev("vector", lambda e: e.tensor_scalar(
            self.sc_b[0:1, 1:2], self.sc_b[0:1, 0:1], 1.0 / n_el, 1e-5,
            ALU.mult, ALU.max), wait=tk)
        tk = ev("vector", lambda e: e.reciprocal(
            self.sc_b[0:1, 2:3], self.sc_b[0:1, 1:2]), wait=tk)
        tk_b = ev("tensor", lambda e: e.matmul(
            self.PB[1][:, 0:2], self.ones_r[:], self.sc_b[0:1, 1:3],
            start=True, stop=True), wait=merge(tk, self.pb_free[1]))
        tok_sc = ev("vector", lambda e: e.tensor_copy(
            self.wsc[:, s_col + 1:s_col + 2], self.PB[1][:, 0:1]), wait=tk_b)
        tok_sc = merge(tok_sc, ev("vector", lambda e: e.tensor_copy(
            self.wsc[:, s_col:s_col + 1], self.PB[1][:, 1:2]), wait=tk_b))
        self.pb_free[0] = tk
        self.pb_free[1] = tok_sc
        ev_busy = self.su_ev_busy0
        for i in range(n_tiles):
            tl = ld(self.su_w[:, 0:tile_cols], w_d[i * P:(i + 1) * P, :],
                    wait=merge(self.last_q3, self.su_w_dma), dsem="dsuw")
            self.su_w_dma = tl
            q1 = ev("vector", lambda e: e.tensor_scalar(
                self.su_q[:, 0:tile_cols], self.su_w[:, 0:tile_cols],
                self.wsc[:, s_col:s_col + 1], MAGIC, ALU.mult, ALU.add),
                wait=merge(tl, tok_sc))
            q2 = ev("vector", lambda e: e.tensor_scalar(
                self.su_b[:, 0:tile_cols], self.su_q[:, 0:tile_cols],
                -MAGIC, None, ALU.add), wait=q1)
            q2 = ev("vector", lambda e: e.scalar_tensor_tensor(
                self.su_e[:, 0:tile_cols], self.su_w[:, 0:tile_cols],
                self.wsc[:, s_col:s_col + 1], self.su_b[:, 0:tile_cols],
                ALU.mult, ALU.subtract), wait=q2)
            q2 = ev("vector", lambda e: e.tensor_scalar(
                self.su_q[:, 0:tile_cols], self.su_e[:, 0:tile_cols],
                0.5, None, ALU.is_gt), wait=q2)
            q2 = ev("vector", lambda e: e.tensor_add(
                self.su_b[:, 0:tile_cols], self.su_b[:, 0:tile_cols],
                self.su_q[:, 0:tile_cols]), wait=q2)
            q2 = ev("vector", lambda e: e.tensor_scalar(
                self.su_q[:, 0:tile_cols], self.su_e[:, 0:tile_cols],
                -0.5, None, ALU.is_lt), wait=q2)
            q2 = ev("vector", lambda e: e.tensor_sub(
                self.su_b[:, 0:tile_cols], self.su_b[:, 0:tile_cols],
                self.su_q[:, 0:tile_cols]), wait=q2)
            q3 = ev("vector", lambda e: e.tensor_scalar(
                self.su_b[:, 0:tile_cols], self.su_b[:, 0:tile_cols],
                -1.0, 1.0, ALU.max, ALU.min), wait=q2)
            self.last_q3 = q3
            for b0 in range(0, NC, 8):
                nb = min(8, NC - b0)
                tts = []
                for j in range(nb):
                    tts.append(ev("tensor", lambda e, jj=b0 + j, j2=j: e.transpose(
                        self.PBB[:, j2 * P:(j2 + 1) * P],
                        self.su_b[:, jj * P:(jj + 1) * P], self.ident_b[:]),
                        wait=merge(q3, self.pbb_free)))
                tev = ev("vector", lambda e, n=nb: e.tensor_copy(
                    self.su_ev[:, 0, 0:n * P], self.PBB[:, 0:n * P]),
                    wait=merge(tts[-1], ev_busy))
                self.pbb_free = tev
                dap = dst[b0:b0 + nb, i]
                src = self.su_ev[:, 0, 0:nb * P].rearrange("p (n c) -> p n c", n=nb)
                ev_busy = self.st(dap.transpose([1, 0, 2]), src,
                                  wait=tev, q="gpsimd", dsem="dwst0")
                self.su_ev_busy0 = ev_busy
                self.wprep_tok = merge(self.wprep_tok, ev_busy)

    def emit_wprep_split(self, w_d, n_rows_tiles, dst_hi, dst_lo):
        ev, ld = self.ev, self.ld
        ev_busy = self.su_ev_busy1
        for i in range(n_rows_tiles):
            tl = ld(self.su_w[:, 0:D], w_d[i * P:(i + 1) * P, :],
                    wait=merge(self.last_q3, self.su_w_dma), dsem="dsuw")
            self.su_w_dma = tl
            h1 = ev("vector", lambda e: e.tensor_copy(
                self.su_b[:, 0:D], self.su_w[:, 0:D]),
                wait=merge(tl, self.tok_consts))
            l1 = ev("vector", lambda e: e.tensor_sub(
                self.su_b[:, D:2 * D], self.su_w[:, 0:D], self.su_b[:, 0:D]),
                wait=h1)
            self.last_q3 = l1
            for pol, src_off, dst_t in (("hi", 0, dst_hi), ("lo", D, dst_lo)):
                tts = []
                for j in range(ND):
                    tts.append(ev("tensor", lambda e, jj=j, so=src_off: e.transpose(
                        self.PBB[:, jj * P:(jj + 1) * P],
                        self.su_b[:, so + jj * P:so + (jj + 1) * P],
                        self.ident_b[:]), wait=merge(l1, self.pbb_free)))
                tev = ev("vector", lambda e: e.tensor_copy(
                    self.su_ev[:, 1, 0:ND * P], self.PBB[:, 0:ND * P]),
                    wait=merge(tts[-1], ev_busy))
                self.pbb_free = tev
                dap = dst_t[:, i]
                src = self.su_ev[:, 1, 0:ND * P].rearrange("p (n c) -> p n c", n=ND)
                ev_busy = self.st(dap.transpose([1, 0, 2]), src,
                                  wait=tev, q="gpsimd", dsem="dwst1")
                self.su_ev_busy1 = ev_busy
                self.wprep_tok = merge(self.wprep_tok, ev_busy)

    def emit_setup(self):
        self.wprep_tok = Tok()
        self.su_w_dma = Tok()
        self.last_q3 = Tok()
        self.su_ev_busy0 = Tok()
        self.su_ev_busy1 = Tok()
        self.pb_free = [Tok() for _ in range(7)]
        self.pbb_free = Tok()
        self.emit_consts()
        if self.slimit >= 1:
            self.emit_wprep_quant(self.w1_d, NF1, D, 0, self.s_w1qT, False)
        if self.slimit >= 2:
            self.emit_wprep_quant(self.w2_d, ND, F1, 2, self.s_w2qT, True)
        if self.slimit >= 3:
            self.emit_wprep_split(self.wqkv_d, NF, self.s_wqkvT["hi"],
                                  self.s_wqkvT["lo"])
        if self.slimit >= 4:
            self.emit_wprep_split(self.wout_d, ND, self.s_woutT["hi"],
                                  self.s_woutT["lo"])

    def emit_ln1(self, s):
        ev, ld = self.ev, self.ld
        eps = 1e-5
        fence = self.pr.fence_all()
        out_toks = []
        busy = [fence, fence]
        pbb_free = self.pr.fence_all()
        for tt in range(NT):
            sl = tt % 2
            tl = ld(self.ln_x[:, sl, :], self.x_d[s, tt * P:(tt + 1) * P, :],
                    wait=busy[sl], dsem=f"dlnx{sl}")
            t_sum = ev("vector", lambda e, sl=sl: e.reduce_sum(
                self.sc_a[:, 0:1], self.ln_x[:, sl, :], axis=AX.X), wait=tl)
            t_sq = ev("scalar", lambda e, sl=sl: e.activation(
                self.ln_xh[:, sl, :], self.ln_x[:, sl, :], AF.Square,
                accum_out=self.sc_a[:, 1:2]), wait=tl)
            tk = ev("vector", lambda e: e.tensor_scalar(
                self.sc_a[:, 2:3], self.sc_a[:, 0:1], 1.0 / D, None, ALU.mult),
                wait=merge(t_sum, t_sq))
            tk = ev("vector", lambda e: e.tensor_tensor(
                self.sc_a[:, 3:4], self.sc_a[:, 2:3], self.sc_a[:, 2:3],
                ALU.mult), wait=tk)
            tk = ev("vector", lambda e: e.scalar_tensor_tensor(
                self.sc_a[:, 4:5], self.sc_a[:, 1:2], 1.0 / D, self.sc_a[:, 3:4],
                ALU.mult, ALU.subtract), wait=tk)
            tk = ev("scalar", lambda e: e.activation(
                self.sc_a[:, 5:6], self.sc_a[:, 4:5], AF.Sqrt,
                bias=self.eps_c[:, 0:1], scale=1.0), wait=tk)
            tk = ev("vector", lambda e, tt=tt: e.reciprocal(
                self.ln_r[:, tt:tt + 1], self.sc_a[:, 5:6]), wait=tk)
            tk = ev("vector", lambda e, tt=tt: e.scalar_tensor_tensor(
                self.ln_m[:, tt:tt + 1], self.sc_a[:, 2:3], -1.0,
                self.ln_r[:, tt:tt + 1], ALU.mult, ALU.mult), wait=tk)
            t_xh = ev("scalar", lambda e, sl=sl, tt=tt: e.activation(
                self.ln_xh[:, sl, :], self.ln_x[:, sl, :], AF.Identity,
                bias=self.ln_m[:, tt:tt + 1], scale=self.ln_r[:, tt:tt + 1]),
                wait=tk)
            t_hi = ev("vector", lambda e, sl=sl: e.tensor_copy(
                self.ln_hi[:, sl, :], self.ln_xh[:, sl, :]), wait=t_xh)
            t_lo = ev("vector", lambda e, sl=sl: e.tensor_sub(
                self.ln_lo[:, sl, :], self.ln_xh[:, sl, :], self.ln_hi[:, sl, :]),
                wait=t_hi)
            stok = None
            for pol, src, tok in (("hi", self.ln_hi, t_hi), ("lo", self.ln_lo, t_lo)):
                tts = []
                for j in range(ND):
                    tts.append(ev("tensor", lambda e, sl=sl, jj=j, sr=src: e.transpose(
                        self.PBB[:, jj * P:(jj + 1) * P],
                        sr[:, sl, jj * P:(jj + 1) * P], self.ident_b[:]),
                        wait=merge(tok, pbb_free)))
                tev = ev("vector", lambda e, sl=sl: e.tensor_copy(
                    self.ln_st[:, sl, :], self.PBB[:, 0:ND * P]),
                    wait=merge(tts[-1], stok))
                pbb_free = tev
                dap = self.s_xhat[pol][s, :, tt]
                srcap = self.ln_st[:, sl, :].rearrange("p (n c) -> p n c", n=ND)
                stok = self.st(dap.transpose([1, 0, 2]), srcap,
                               wait=tev, q="gpsimd", dsem="dxh")
                out_toks.append(stok)
            busy[sl] = merge(t_sum, t_sq, t_xh, stok)
        self.xhat_tok = merge(*out_toks)
        if s == 0:
            d = self.dbg_out("xhat_hi0", [ND, NT, P, P], BF16)
            if d is not None:
                self.st(d, self.s_xhat["hi"][0], wait=self.xhat_tok, dsem="ddbg")
            d = self.dbg_out("xhat_lo0", [ND, NT, P, P], BF16)
            if d is not None:
                self.st(d, self.s_xhat["lo"][0], wait=self.xhat_tok, dsem="ddbg")

    def emit_qkv(self, s, sg):
        ev, ld = self.ev, self.ld
        fence = self.pr.fence_all()
        fk, fq, fv = 6 * sg + 4, 6 * sg + 0, 6 * sg + 2

        mx_z = ev("vector", lambda e: e.memset(self.mx_acc[:], 0.0), wait=fence)
        self.attnsb_free = fence
        xst_busy = [fence] * 8
        wstage_busy = [fence, fence]
        ring = 0
        kqf_done = []
        v_done = []
        kq_free = [fence] * 4
        v_free = [fence, fence]
        pb6_free = fence
        rot_busy = [fence] * 2

        for tb in range(4):
            ts_ = [4 * tb + i for i in range(4)]
            mm_last = {}
            for d in range(ND):
                wsl = d % 2
                wtoks = []
                for pol_i, pol in enumerate(("hi", "lo")):
                    src_t = self.s_wqkvT[pol]
                    for kk, f0 in ((0, fk), (256, fq), (512, fv)):
                        wtoks.append(self.st(
                            self.wq_stage[:, wsl, pol_i, kk:kk + 256],
                            src_t[d, f0:f0 + 2].transpose([1, 0, 2]),
                            wait=merge(self.wprep_tok, wstage_busy[wsl]),
                            q="sync", dsem=f"dwq{wsl}"))
                tok_w = merge(*wtoks)
                tv = None
                for ti, t in enumerate(ts_):
                    hi_slot = (ring * 2) % 8
                    lo_slot = (ring * 2 + 1) % 8
                    ring += 1
                    x_hi = self.st(self.xst_ring[:, hi_slot, :],
                                   self.s_xhat["hi"][s, d, t],
                                   wait=merge(self.xhat_tok, xst_busy[hi_slot]),
                                   q="sync", dsem=f"dxs{hi_slot}")
                    x_lo = self.st(self.xst_ring[:, lo_slot, :],
                                   self.s_xhat["lo"][s, d, t],
                                   wait=merge(self.xhat_tok, xst_busy[lo_slot]),
                                   q="sync", dsem=f"dxs{lo_slot}")
                    first = d == 0
                    last = d == ND - 1
                    pkq = self.PB[ti]
                    pv = self.PB[4 + ti // 2][:, (ti % 2) * 256:(ti % 2) * 256 + 256]
                    wt = merge(tok_w, x_hi, x_lo)
                    if first:
                        wt = merge(wt, kq_free[ti], v_free[ti // 2])
                    passes = [(hi_slot, 0), (hi_slot, 1), (lo_slot, 0)]
                    tkq = None
                    for pi, (xslot, wpol) in enumerate(passes):
                        stf = first and pi == 0
                        stf_v = stf and (ti % 2 == 0)
                        spf = last and pi == 2
                        tkq = ev("tensor", lambda e, xs_=xslot, wp=wpol, sf=stf,
                                 ef=spf, pk=pkq, ws=wsl: e.matmul(
                            pk[:, 0:512], self.xst_ring[:, xs_, :],
                            self.wq_stage[:, ws, wp, 0:512],
                            start=sf, stop=ef, skip_group_check=True), wait=wt)
                        tv = ev("tensor", lambda e, xs_=xslot, wp=wpol, sf=stf_v,
                                ef=spf, pv_=pv, ws=wsl: e.matmul(
                            pv_, self.xst_ring[:, xs_, :],
                            self.wq_stage[:, ws, wp, 512:768],
                            start=sf, stop=ef, skip_group_check=True))
                        wt = None
                    xst_busy[hi_slot] = tv
                    xst_busy[lo_slot] = tv
                    if last:
                        mm_last[t] = (ti, tkq, tv)
                wstage_busy[wsl] = tv
            for t, (ti, tkq, tv) in mm_last.items():
                sl = t % 2
                pkq = self.PB[ti]
                pv = self.PB[4 + ti // 2][:, (ti % 2) * 256:(ti % 2) * 256 + 256]
                partner = ti ^ 1
                tv_partner = mm_last[[tt for tt, (tj, _, _) in mm_last.items()
                                      if tj == partner][0]][2]
                t_ve = ev("vector", lambda e, t=t, pv_=pv: e.tensor_copy(
                    self.v_tok[:, t, :], pv_),
                    wait=merge(tv, tkq, tv_partner))
                v_free[ti // 2] = merge(v_free[ti // 2], t_ve)
                v_done.append(t_ve)
                pk3 = pkq[:, 0:512].rearrange("p (g c) -> p g c", g=8)
                rot3 = self.kq_rot[:, sl, :].rearrange("p (g c) -> p g c", g=8)
                ct = self.cos_s[:, t * 16:(t + 1) * 16]
                st_ = self.sin_s[:, t * 16:(t + 1) * 16]
                cb = ct.rearrange("p (o f) -> p o f", o=1).broadcast_to([P, 8, 16])
                sb = st_.rearrange("p (o f) -> p o f", o=1).broadcast_to([P, 8, 16])
                tmp_e = self.rope_tmp[:, 0, :].rearrange("p (g c) -> p g c", g=8)
                tmp_o = self.rope_tmp[:, 1, :].rearrange("p (g c) -> p g c", g=8)
                t1 = ev("vector", lambda e, a=pk3, b=cb, o=tmp_e: e.tensor_tensor(
                    o, a[:, :, 0:32:2], b, ALU.mult),
                    wait=merge(tkq, tv, self.tok_consts))
                t2 = ev("vector", lambda e, a=pk3, b=sb, o=tmp_o: e.tensor_tensor(
                    o, a[:, :, 1:32:2], b, ALU.mult))
                t3 = ev("vector", lambda e, a=tmp_e, b=tmp_o, o=rot3, sl=sl:
                        e.tensor_tensor(o[:, :, 0:32:2], a, b, ALU.subtract),
                        wait=rot_busy[sl])
                t4 = ev("vector", lambda e, a=pk3, b=cb, o=tmp_e: e.tensor_tensor(
                    o, a[:, :, 1:32:2], b, ALU.mult))
                t5 = ev("vector", lambda e, a=pk3, b=sb, o=tmp_o: e.tensor_tensor(
                    o, a[:, :, 0:32:2], b, ALU.mult))
                t6 = ev("vector", lambda e, a=tmp_e, b=tmp_o, o=rot3:
                        e.tensor_tensor(o[:, :, 1:32:2], a, b, ALU.add))
                t7 = ev("vector", lambda e, a=pk3, o=rot3: e.tensor_copy(
                    o[:, :, 32:64], a[:, :, 32:64]))
                kq_free[ti] = t7
                sqv = self.attn_sb[:, 0, :].rearrange("p (g c) -> p g c", g=8)
                t8a = ev("vector", lambda e, o=rot3, sq=sqv: e.tensor_tensor(
                    sq, o, o, ALU.mult), wait=merge(t6, t7, self.attnsb_free))
                t8b = ev("vector", lambda e, sq=sqv: e.reduce_sum(
                    self.sc_b[:, 0:8], sq, axis=AX.X), wait=t8a)
                self.attnsb_free = t8b
                t9 = ev("vector", lambda e: e.tensor_tensor(
                    self.mx_acc[:, 0:4], self.mx_acc[:, 0:4], self.sc_b[:, 0:4],
                    ALU.max), wait=merge(t8b, mx_z))
                t9b = ev("vector", lambda e, t=t: e.tensor_copy(
                    self.qn2[:, t * 4:(t + 1) * 4], self.sc_b[:, 4:8]),
                    wait=t8b)
                tts = []
                for c in range(4):
                    tts.append(ev("tensor", lambda e, sl=sl, c=c: e.transpose(
                        self.PB[6][:, c * P:(c + 1) * P],
                        self.kq_rot[:, sl, c * P:(c + 1) * P],
                        self.ident_f[:]), wait=merge(t7, pb6_free)))
                tfe = ev("vector", lambda e, t=t: e.tensor_copy(
                    self.kqf[:, :, t * P:(t + 1) * P],
                    self.PB[6][:, 0:512].rearrange("p (n c) -> p n c", n=4)),
                    wait=tts[-1])
                pb6_free = tfe
                rot_busy[sl] = tfe
                kqf_done.append(tfe)
        tm = ev("tensor", lambda e: e.transpose(
            self.PB[0][0:4, 0:P], self.mx_acc[:, 0:4], self.ident_f[:]),
            wait=merge(merge(*kqf_done), kq_free[0]))
        tr = ev("vector", lambda e: e.reduce_max(
            self.sc_b[0:4, 0:1], self.PB[0][0:4, 0:P], axis=AX.X), wait=tm)
        tm2 = ev("tensor", lambda e: e.transpose(
            self.PB[1][0:1, 0:4], self.sc_b[0:4, 0:1], self.ident_f[0:4, 0:4]),
            wait=merge(tr, kq_free[1]))
        tev = ev("vector", lambda e: e.tensor_copy(
            self.qrow[0:1, 0:4], self.PB[1][0:1, 0:4]), wait=tm2)
        tb2 = ev("tensor", lambda e: e.matmul(
            self.PB[1][:, 4:8], self.ones_r[:], self.qrow[0:1, 0:4],
            start=True, stop=True), wait=tev)
        tok_mk2 = ev("vector", lambda e: e.tensor_copy(
            self.m0_bc[:], self.PB[1][:, 4:8]), wait=tb2)
        q3 = self.qn2[:].rearrange("p (t h) -> p t h", t=NT)
        m3 = self.m0_bc[:].rearrange("p (o h) -> p o h", o=1).broadcast_to(
            [P, NT, 4])
        b3 = self.mbias[:].rearrange("p (t h) -> p t h", t=NT)
        tok_b = ev("vector", lambda e: e.tensor_tensor(
            b3, q3, m3, ALU.mult), wait=merge(tok_mk2, t9b))
        tok_b = ev("scalar", lambda e: e.activation(
            self.mbias[:], self.mbias[:], AF.Sqrt), wait=tok_b)
        tok_b = ev("vector", lambda e: e.tensor_scalar(
            self.mbias[:], self.mbias[:], -0.125, None, ALU.mult), wait=tok_b)
        self.tok_m0 = tok_b
        self.tok_kqf = merge(merge(*kqf_done), merge(*v_done), self.tok_m0)
        if s == 0 and sg == 0:
            d = self.dbg_out("kqf0", [P, 4, S])
            if d is not None:
                self.st(d, self.kqf[:], wait=self.tok_kqf, dsem="ddbg")
            d = self.dbg_out("vtok0", [P, NT, 256])
            if d is not None:
                self.st(d, self.v_tok[:], wait=self.tok_kqf, dsem="ddbg")
            d = self.dbg_out("m00", [P, 4])
            if d is not None:
                self.st(d, self.m0_bc[:], wait=self.tok_kqf, dsem="ddbg")

    def emit_attention(self, s, sg):
        ev = self.ev
        fence = self.pr.fence_all()
        heads_toks = []
        sums_z = ev("vector", lambda e: e.memset(self.sums_c[:], 0.0),
                    wait=fence)
        s_free = [fence] * 4
        t_free = [fence, fence]
        av_free = fence
        asb_busy = [fence] * 2
        atsb_busy = [fence] * 2
        for pair in range(2):
            for qc in range(NT):
                nkb = qc // 4 + 1
                exp_toks = {0: [], 1: []}
                av_last = None
                slot_i = 0
                for kb in range(nkb):
                    klen = min(512, (qc + 1) * P - kb * 512)
                    nch = (klen + P - 1) // P
                    for hh in range(2):
                        hl = 2 * pair + hh
                        sbank_i = 2 * hh + (kb % 2)
                        sbank = self.PB[sbank_i]
                        tq = ev("tensor", lambda e, hh=hh, pair=pair, qc=qc,
                                kb=kb, kl=klen, sb=sbank: e.matmul(
                            sb[:, 0:kl],
                            self.kqf[64 * hh:64 * hh + 64, 2 + pair,
                                     qc * P:(qc + 1) * P],
                            self.kqf[64 * hh:64 * hh + 64, pair,
                                     kb * 512:kb * 512 + kl],
                            start=True, stop=True,
                            tile_position=(64 * hh, 0), skip_group_check=True),
                            wait=merge(self.tok_kqf, s_free[sbank_i]))
                        asl = slot_i % 2
                        col = hh * 8 + kb
                        tq2 = tq
                        if kb == qc // 4:
                            r = qc - 4 * kb
                            tq2 = ev("vector", lambda e, sb=sbank, r=r, kl=klen:
                                     e.tensor_add(
                                sb[:, 0:kl], sb[:, 0:kl],
                                self.mask_s[:, 512 * r:512 * r + kl]),
                                wait=merge(tq, self.tok_consts))
                        texp = ev("scalar", lambda e, sb=sbank, asl=asl, hl=hl,
                                  col=col, kl=klen: e.activation(
                            self.attn_sb[:, asl, 0:kl], sb[:, 0:kl], AF.Exp,
                            bias=self.mbias[:, qc * 4 + hl:qc * 4 + hl + 1],
                            scale=0.125,
                            accum_out=self.sums_c[:, col:col + 1]),
                            wait=merge(tq2, sums_z, asb_busy[asl]))
                        s_free[sbank_i] = texp
                        exp_toks[hh].append(texp)
                        tmask = texp
                        tbank = self.PB[4 + hh]
                        tts = []
                        for c in range(nch):
                            tts.append(ev("tensor", lambda e, asl=asl, c=c,
                                          tb=tbank: e.transpose(
                                tb[:, c * P:(c + 1) * P],
                                self.attn_sb[:, asl, c * P:(c + 1) * P],
                                self.ident_f[:]),
                                wait=merge(tmask, t_free[hh])))
                        atsl = slot_i % 2
                        tev = ev("vector", lambda e, atsl=atsl, tb=tbank, kl=klen:
                                 e.tensor_copy(
                            self.attnT_sb[:, atsl, 0:kl], tb[:, 0:kl]),
                            wait=merge(tts[-1], atsb_busy[atsl]))
                        t_free[hh] = tev
                        asb_busy[asl] = tev
                        for c in range(nch):
                            stf = (kb == 0 and c == 0 and hh == 0)
                            spf = (kb == nkb - 1 and c == nch - 1)
                            av_last = ev("tensor", lambda e, atsl=atsl, c=c,
                                         hh=hh, hl=hl, kb=kb, stf=stf, spf=spf:
                                         e.matmul(
                                self.PB[6][:, 64 * hh:64 * hh + 64],
                                self.attnT_sb[:, atsl, c * P:(c + 1) * P],
                                self.v_tok[:, kb * 4 + c, 64 * hl:64 * hl + 64],
                                start=stf, stop=spf, skip_group_check=True),
                                wait=merge(tev, av_free if stf and hh == 0
                                           and kb == 0 else None))
                            atsb_busy[atsl] = av_last
                        slot_i += 1
                hev = []
                for hh in range(2):
                    hl = 2 * pair + hh
                    hg = 4 * sg + hl
                    tsum = ev("vector", lambda e, hh=hh, nkb=nkb: e.reduce_sum(
                        self.recip_c[:, 2 + hh:3 + hh],
                        self.sums_c[:, hh * 8:hh * 8 + nkb], axis=AX.X),
                        wait=merge(*exp_toks[hh]))
                    trc = ev("vector", lambda e, hh=hh: e.reciprocal(
                        self.recip_c[:, hh:hh + 1],
                        self.recip_c[:, 2 + hh:3 + hh]), wait=tsum)
                    hev.append(ev("vector", lambda e, qc=qc, hg=hg, hh=hh:
                                  e.tensor_scalar(
                        self.heads_t[:, qc, 64 * hg:64 * hg + 64],
                        self.PB[6][:, 64 * hh:64 * hh + 64],
                        self.recip_c[:, hh:hh + 1], None, ALU.mult),
                        wait=merge(trc, av_last)))
                av_free = merge(*hev)
                sums_z = ev("vector", lambda e: e.memset(self.sums_c[:], 0.0),
                            wait=av_free)
                heads_toks.extend(hev)
        self.tok_heads = merge(self.tok_heads, *heads_toks)

    def emit_wout(self, s):
        ev, ld = self.ev, self.ld
        fence = self.pr.fence_all()
        x1_toks = []
        hfw_busy = fence
        t_free = [fence, fence]
        ao_psum_free = [fence, fence]
        pb6_free = fence
        ao_ev_busy = [fence, fence]
        wo_busy = [fence] * 6
        xres_busy = [fence] * 4
        for w in range(4):
            hfw_toks = []
            for ti in range(4):
                tt = 4 * w + ti
                tts = {}
                for j in range(ND):
                    half = j // 4
                    bank = self.PB[4 + half]
                    tts[j] = ev("tensor", lambda e, tt=tt, j=j, b=bank:
                                e.transpose(
                        b[:, (j % 4) * P:(j % 4 + 1) * P],
                        self.heads_t[:, tt, j * P:(j + 1) * P],
                        self.ident_f[:]),
                        wait=merge(self.tok_heads, t_free[half], hfw_busy))
                for half in range(2):
                    bank = self.PB[4 + half]
                    tk = tts[half * 4 + 3]
                    dsth = self.hfw[:, 0, half * 4:half * 4 + 4,
                                    ti * P:(ti + 1) * P]
                    dstl = self.hfw[:, 1, half * 4:half * 4 + 4,
                                    ti * P:(ti + 1) * P]
                    srcb = bank[:, 0:512].rearrange("p (n c) -> p n c", n=4)
                    th = ev("vector", lambda e, d=dsth, sr=srcb: e.tensor_copy(
                        d, sr), wait=tk)
                    tl = ev("vector", lambda e, d=dstl, sr=srcb, hi=dsth:
                            e.tensor_tensor(d, sr, hi, ALU.subtract), wait=th)
                    t_free[half] = tl
                    hfw_toks.append(tl)
            tok_hfw = merge(*hfw_toks)
            xtoks = []
            if self.wlimit >= 2:
                for ti in range(4):
                    tt = 4 * w + ti
                    xtoks.append(self.ld(self.xres[:, ti, :],
                                         self.x_d[s, tt * P:(tt + 1) * P, :],
                                         wait=xres_busy[ti], dsem=f"dxr{ti}"))
            else:
                xtoks = [Tok()] * 4
            if self.wlimit < 3:
                continue
            new_hfw_busy = []
            for m in range(ND):
                pbank = self.PB[m % 2]
                mm_tok = None
                for k in range(ND):
                    ws = (m * ND + k) % 6
                    wts = [self.st(self.wo_stage[:, ws, pol_i, :],
                                   self.s_woutT[pol][k, m],
                                   wait=merge(self.wprep_tok, wo_busy[ws]),
                                   q="sync", dsem=f"dwo{ws}")
                           for pol_i, pol in enumerate(("hi", "lo"))]
                    wt = merge(*wts, tok_hfw)
                    if k == 0:
                        wt = merge(wt, ao_psum_free[m % 2])
                    passes = [(0, 0), (0, 1), (1, 0)]
                    for pi, (hp, wp) in enumerate(passes):
                        stf = (k == 0 and pi == 0)
                        spf = (k == ND - 1 and pi == 2)
                        mm_tok = ev("tensor", lambda e, k=k, hp=hp, wp=wp,
                                    ws=ws, pb=pbank, stf=stf, spf=spf: e.matmul(
                            pb[:, 0:512],
                            self.wo_stage[:, ws, wp, :],
                            self.hfw[:, hp, k, :],
                            start=stf, stop=spf, skip_group_check=True),
                            wait=wt)
                        wt = None
                    wo_busy[ws] = mm_tok
                new_hfw_busy.append(mm_tok)
                if self.wlimit < 4:
                    ao_psum_free[m % 2] = mm_tok
                    continue
                tae = ev("vector", lambda e, m=m, pb=pbank: e.tensor_copy(
                    self.ao_ev[:, m % 2, :], pb[:, 0:512]),
                    wait=merge(mm_tok, ao_ev_busy[m % 2]))
                ao_psum_free[m % 2] = tae
                if self.wlimit < 4.5:
                    ao_ev_busy[m % 2] = tae
                    continue
                ttrs = []
                for ti in range(4):
                    ttrs.append(ev("tensor", lambda e, m=m, ti=ti: e.transpose(
                        self.PB[6][:, ti * P:(ti + 1) * P],
                        self.ao_ev[:, m % 2, ti * P:(ti + 1) * P],
                        self.ident_f[:]), wait=merge(tae, pb6_free)))
                ttr_all = ttrs[-1]
                adds = []
                for ti in range(4):
                    tt = 4 * w + ti
                    tadd = ev("vector", lambda e, m=m, tt=tt, ti=ti:
                              e.tensor_add(
                        self.x1_t[:, tt, m * P:(m + 1) * P],
                        self.PB[6][:, ti * P:(ti + 1) * P],
                        self.xres[:, ti, :][:, m * P:(m + 1) * P]),
                        wait=merge(ttrs[ti] if self.wlimit == 7 else ttr_all,
                                   xtoks[ti]))
                    adds.append(tadd)
                    x1_toks.append(tadd)
                pb6_free = merge(*adds)
                ao_ev_busy[m % 2] = pb6_free
            hfw_busy = merge(*new_hfw_busy)
            xres_busy = [merge(*x1_toks[-4 * ND:])] * 4
        self.tok_x1 = merge(*x1_toks)
        if s == 0:
            d = self.dbg_out("x1_0", [P, NT, D])
            if d is not None:
                self.st(d, self.x1_t[:], wait=self.tok_x1, dsem="ddbg")
            d = self.dbg_out("heads0", [P, NT, D])
            if d is not None:
                self.st(d, self.heads_t[:], wait=self.tok_heads, dsem="ddbg")

    def emit_ln2(self, s):
        ev = self.ev
        eps = 1e-5
        toks = []
        for tt in range(NT):
            t_sum = ev("vector", lambda e, tt=tt: e.reduce_sum(
                self.sc_a[:, 0:1], self.x1_t[:, tt, :], axis=AX.X),
                wait=self.tok_x1)
            t_sq = ev("scalar", lambda e, tt=tt: e.activation(
                self.x1h_ring[:, 0, :], self.x1_t[:, tt, :], AF.Square,
                accum_out=self.sc_a[:, 1:2]), wait=self.tok_x1)
            tk = ev("vector", lambda e: e.tensor_scalar(
                self.sc_a[:, 2:3], self.sc_a[:, 0:1], 1.0 / D, None, ALU.mult),
                wait=merge(t_sum, t_sq))
            tk = ev("vector", lambda e: e.tensor_tensor(
                self.sc_a[:, 3:4], self.sc_a[:, 2:3], self.sc_a[:, 2:3],
                ALU.mult), wait=tk)
            tk = ev("vector", lambda e: e.scalar_tensor_tensor(
                self.sc_a[:, 4:5], self.sc_a[:, 1:2], 1.0 / D, self.sc_a[:, 3:4],
                ALU.mult, ALU.subtract), wait=tk)
            tk = ev("scalar", lambda e: e.activation(
                self.sc_a[:, 5:6], self.sc_a[:, 4:5], AF.Sqrt,
                bias=self.eps_c[:, 0:1], scale=1.0), wait=tk)
            tk = ev("vector", lambda e, tt=tt: e.reciprocal(
                self.ln2_r[:, tt:tt + 1], self.sc_a[:, 5:6]), wait=tk)
            toks.append(ev("vector", lambda e, tt=tt: e.scalar_tensor_tensor(
                self.ln2_m[:, tt:tt + 1], self.sc_a[:, 2:3], -1.0,
                self.ln2_r[:, tt:tt + 1], ALU.mult, ALU.mult), wait=tk))
        self.tok_ln2 = merge(*toks)

    def emit_ffn_w1(self, s, w):
        ev = self.ev
        fence = self.pr.fence_all()
        if w == 0:
            self.x1h_busy = [fence]
            self.w12_busy = [fence] * 4
            self.h_psum_free = [fence, fence]
            self.h_busy = [fence] * NF1
            self.pbb_free2 = fence
            self.pb6_free2 = fence
            self.y_psum_free = [fence, fence]
            self.ao2_busy = [fence]
            self.prev2_tok = [None] * NF1
            self.hq_busy = [fence] * NF1
            self.qf16_busy = fence
            self.xqb_busy = fence
            self.acc_busy = fence
            self.yw_busy = fence
            self.y_dma_last = Tok()
            self.pb3_free = fence
            self.pb2_free = fence
            self.qrow_tail_busy = fence
        xq_toks = []
        for ti in range(4):
            tt = 4 * w + ti
            xsl = 0
            t_xh = ev("scalar", lambda e, tt=tt, xsl=xsl: e.activation(
                self.x1h_ring[:, xsl, :], self.x1_t[:, tt, :], AF.Identity,
                bias=self.ln2_m[:, tt:tt + 1], scale=self.ln2_r[:, tt:tt + 1]),
                wait=merge(self.tok_ln2, self.x1h_busy[xsl]))
            t_am = ev("vector", lambda e, xsl=xsl: e.reduce_max(
                self.sc_a[:, 6:7], self.x1h_ring[:, xsl, :], axis=AX.X,
                apply_absolute_value=True), wait=t_xh)
            t_am = ev("vector", lambda e: e.tensor_scalar(
                self.sc_a[:, 7:8], self.sc_a[:, 6:7], 1e-5, None, ALU.max),
                wait=t_am)
            t_s = ev("vector", lambda e, tt=tt: e.tensor_scalar(
                self.rs1tok[:, tt:tt + 1], self.sc_a[:, 7:8], 1.0 / 127.0,
                None, ALU.mult), wait=t_am)
            t_s2 = ev("vector", lambda e, tt=tt: e.reciprocal(
                self.s1tok[:, tt:tt + 1], self.rs1tok[:, tt:tt + 1]),
                wait=t_s)
            tq1 = ev("vector", lambda e, tt=tt, xsl=xsl: e.tensor_scalar(
                self.qf16[:, 0:D], self.x1h_ring[:, xsl, :],
                self.s1tok[:, tt:tt + 1], MAGIC, ALU.mult, ALU.add),
                wait=merge(t_s2, self.qf16_busy))
            tq2 = ev("vector", lambda e: e.tensor_scalar(
                self.xqb_st[:, :], self.qf16[:, 0:D],
                -MAGIC, None, ALU.add), wait=merge(tq1, self.xqb_busy))
            self.qf16_busy = tq2
            self.x1h_busy[xsl] = tq1
            tts = []
            for j in range(ND):
                tts.append(ev("tensor", lambda e, j=j: e.transpose(
                    self.PBB[:, j * P:(j + 1) * P],
                    self.xqb_st[:, j * P:(j + 1) * P], self.ident_b[:]),
                    wait=merge(tq2, self.pbb_free2)))
            tc = ev("vector", lambda e, ti=ti: e.tensor_copy(
                self.xq_win[:, :, ti * P:(ti + 1) * P],
                self.PBB[:, 0:ND * P].rearrange("p (n c) -> p n c", n=ND)),
                wait=tts[-1])
            self.pbb_free2 = tc
            self.xqb_busy = tc
            xq_toks.append(tc)
        tok_xq = merge(*xq_toks)
        tt0 = 4 * w
        tmts = []
        for ti in range(4):
            tmts.append(ev("tensor", lambda e, tt0=tt0, ti=ti: e.transpose(
                self.PB[3][0:1, ti * P:(ti + 1) * P],
                self.rs1tok[:, tt0 + ti:tt0 + ti + 1], self.ident_f[:]),
                wait=merge(tok_xq, self.pb3_free)))
        bts = []
        rb = Tok()
        tcs = []
        for ti in range(4):
            tc = ev("vector", lambda e, ti=ti: e.tensor_copy(
                self.qrow[0:1, 0:P], self.PB[3][0:1, ti * P:(ti + 1) * P]),
                wait=merge(tmts[-1], rb))
            tcs.append(tc)
            mmb = ev("tensor", lambda e, ti=ti: e.matmul(
                self.PB[6][:, ti * P:(ti + 1) * P], self.ones_r[:],
                self.qrow[0:1, 0:P], start=True, stop=True),
                wait=merge(tc, self.pb6_free2 if ti == 0 else Tok()))
            rb = mmb
            bts.append(mmb)
        self.pb3_free = merge(*tcs)
        tok_bc = ev("vector", lambda e: e.tensor_scalar(
            self.bc_w1[:, 0:512], self.PB[6][:, 0:512],
            self.wsc[:, 1:2], None, ALU.mult), wait=merge(*bts))
        self.pb6_free2 = tok_bc
        h_toks = []
        for fc in range(NF1):
            wsl = fc % 2
            wt = self.st(
                self.w12_ring[:, wsl, :, :],
                self.s_w1qT[:, fc].transpose([1, 0, 2]),
                wait=merge(self.wprep_tok, self.w12_busy[wsl]), q="gpsimd",
                dsem=f"dw12_{wsl}")
            pb = self.PB[fc % 2]
            mm = None
            for d in range(ND):
                mm = ev("tensor", lambda e, d=d, pb=pb, wsl=wsl: e.matmul(
                    pb[:, 0:512], self.w12_ring[:, wsl, d, :],
                    self.xq_win[:, d, :],
                    start=(d == 0), stop=(d == ND - 1),
                    skip_group_check=True),
                    wait=merge(wt, tok_xq,
                               self.h_psum_free[fc % 2] if d == 0 else None))
            self.w12_busy[wsl] = mm
            the = ev("vector", lambda e, fc=fc, pb=pb: e.tensor_tensor(
                self.h_t[:, fc, 2:514], pb[:, 0:512], self.bc_w1[:, 0:512],
                ALU.mult), wait=merge(mm, tok_bc, self.h_busy[fc]))
            self.h_psum_free[fc % 2] = the
            self.h_busy[fc] = the
            h_toks.append(the)
        self.tok_h = merge(*h_toks)
        if s == 1 and w == 3:
            d = self.dbg_out("hraw", [P, NF1, 512])
            if d is not None:
                tokd = self.st(d, self.h_t[:, :, 2:514], wait=self.tok_h,
                               dsem="ddbg")
                self.tok_h = merge(self.tok_h, tokd)
                self.h_busy = [merge(t, tokd) for t in self.h_busy]
            d2 = self.dbg_out("bcw1", [P, 512])
            if d2 is not None:
                tk2 = self.st(d2, self.bc_w1[:, 0:512], wait=self.tok_h,
                              dsem="ddbg")
                self.tok_h = merge(self.tok_h, tk2)
                self.t5_free = merge(getattr(self, "t5_free", Tok()), tk2)

    def emit_conv(self, s, w):
        ev = self.ev
        last = (w == 3)
        hs_toks = []
        acc_busy = self.acc_busy
        for fc in range(NF1):
            htile = self.h_t[:, fc, :]
            hb = self.h_busy[fc]
            if w == 0:
                tz = ev("vector", lambda e, ht=htile: e.memset(ht[:, 0:2], 0.0),
                        wait=hb)
            else:
                tz = ev("vector", lambda e, ht=htile, fc=fc: e.tensor_copy(
                    ht[:, 0:2], self.prev2[:, 2 * fc:2 * fc + 2]),
                    wait=merge(hb, self.prev2_tok[fc]))
            if last:
                tz = merge(tz, ev("vector", lambda e, ht=htile: e.memset(
                    ht[:, 514:515], 0.0), wait=hb))
            tp = ev("vector", lambda e, ht=htile, fc=fc: e.tensor_copy(
                self.prev2[:, 2 * fc:2 * fc + 2], ht[:, 512:514]), wait=tz)
            self.prev2_tok[fc] = tp
            ncen = 513 if last else 512
            w0 = self.convw_s[:, 3 * fc:3 * fc + 1]
            w1c = self.convw_s[:, 3 * fc + 1:3 * fc + 2]
            w2c = self.convw_s[:, 3 * fc + 2:3 * fc + 3]
            t1 = ev("vector", lambda e, ht=htile, n=ncen, w0=w0: e.tensor_scalar(
                self.acc_c[:, 0:n], ht[:, 0:n], w0, None, ALU.mult),
                wait=merge(tp, self.tok_consts, acc_busy, self.ao2_busy[0]))
            t2 = ev("vector", lambda e, ht=htile, n=ncen, w1c=w1c:
                    e.scalar_tensor_tensor(
                self.acc_c[:, 0:n], ht[:, 1:1 + n], w1c, self.acc_c[:, 0:n],
                ALU.mult, ALU.add), wait=t1)
            t3 = ev("vector", lambda e, ht=htile, n=ncen, w2c=w2c:
                    e.scalar_tensor_tensor(
                self.acc_c[:, 0:n], ht[:, 2:2 + n], w2c, self.acc_c[:, 0:n],
                ALU.mult, ALU.add), wait=t2)
            tsl = ev("scalar", lambda e, ht=htile, n=ncen, fc=fc: e.activation(
                ht[:, 1:1 + n], self.acc_c[:, 0:n], AF.Silu,
                bias=self.convb_s[:, fc:fc + 1], scale=1.0), wait=t3)
            acc_busy = tsl
            self.h_busy[fc] = tsl
            hs_toks.append(tsl)
        self.acc_busy = acc_busy
        self.tok_hs = merge(*hs_toks)

    def emit_quant2_w2(self, s, w):
        ev = self.ev
        col0 = 2 if w == 0 else 1
        ncols = (514 if w == 3 else 513) - col0
        tok0 = 0 if w == 0 else 512 * w - 1
        am = ev("vector", lambda e, c0=col0, n=ncols: e.tensor_copy(
            self.acc_c[:, 0:n], self.h_t[:, 0, c0:c0 + n]),
            wait=merge(self.tok_hs, self.acc_busy))
        am = ev("vector", lambda e, c0=col0, n=ncols: e.tensor_copy(
            self.acc2[:, 0:n], self.h_t[:, 0, c0:c0 + n]), wait=am)
        for fc in range(1, NF1):
            am = ev("vector", lambda e, fc=fc, c0=col0, n=ncols: e.tensor_tensor(
                self.acc_c[:, 0:n], self.acc_c[:, 0:n],
                self.h_t[:, fc, c0:c0 + n], ALU.max), wait=am)
            am = ev("vector", lambda e, fc=fc, c0=col0, n=ncols: e.tensor_tensor(
                self.acc2[:, 0:n], self.acc2[:, 0:n],
                self.h_t[:, fc, c0:c0 + n], ALU.min), wait=am)
        am = ev("vector", lambda e, n=ncols: e.scalar_tensor_tensor(
            self.acc_c[:, 0:n], self.acc2[:, 0:n], -1.0, self.acc_c[:, 0:n],
            ALU.mult, ALU.max), wait=am)
        nti = (ncols + P - 1) // P
        s2_toks = []
        zz = ev("vector", lambda e: e.memset(self.s2tok[:], 1.0))
        zz = ev("vector", lambda e: e.memset(self.rs2tok[:], 1.0), wait=zz)
        prev_tr = zz
        for ti in range(nti):
            cw = min(P, ncols - ti * P)
            tt1 = ev("tensor", lambda e, ti=ti, cw=cw: e.transpose(
                self.PB[6][0:cw, 0:P],
                self.acc_c[:, ti * P:ti * P + cw],
                self.ident_f[:]), wait=merge(am, self.pb6_free2, prev_tr))
            tr1 = ev("vector", lambda e, ti=ti, cw=cw: e.reduce_max(
                self.s2tok[0:cw, ti:ti + 1],
                self.PB[6][0:cw, 0:P], axis=AX.X), wait=merge(tt1, zz))
            tr1 = ev("vector", lambda e, ti=ti: e.tensor_scalar(
                self.s2tok[:, ti:ti + 1], self.s2tok[:, ti:ti + 1], 1e-5,
                None, ALU.max), wait=tr1)
            tr1 = ev("vector", lambda e, ti=ti: e.tensor_scalar(
                self.rs2tok[:, ti:ti + 1], self.s2tok[:, ti:ti + 1],
                1.0 / 127.0, None, ALU.mult), wait=tr1)
            s2_toks.append(ev("vector", lambda e, ti=ti: e.reciprocal(
                self.s2tok[:, ti:ti + 1], self.rs2tok[:, ti:ti + 1]),
                wait=tr1))
            prev_tr = s2_toks[-1]
        tok_s2 = merge(*s2_toks)
        self.pb6_free2 = tok_s2
        if self.qlimit < 2:
            return
        tm_s, tm_r = [], []
        for ti in range(min(nti, 4)):
            tm_s.append(ev("tensor", lambda e, ti=ti: e.transpose(
                self.PB[3][0:1, ti * P:(ti + 1) * P],
                self.s2tok[:, ti:ti + 1], self.ident_f[:]),
                wait=merge(tok_s2, self.pb3_free)))
            tm_r.append(ev("tensor", lambda e, ti=ti: e.transpose(
                self.PB[2][0:1, ti * P:(ti + 1) * P],
                self.rs2tok[:, ti:ti + 1], self.ident_f[:]),
                wait=merge(tok_s2, self.pb2_free)))
        tmt2 = merge(*tm_s, *tm_r)
        rb = Tok()
        bq, br = [], []
        tcs2 = []
        for ti in range(min(nti, 4)):
            cw = min(P, ncols - ti * P)
            for which in range(2):
                srccol = 0 if which == 0 else 128
                pbsrc = self.PB[3] if which == 0 else self.PB[2]
                tc = ev("vector", lambda e, ti=ti, pbs=pbsrc: e.tensor_copy(
                    self.qrow[0:1, 0:P], pbs[0:1, ti * P:(ti + 1) * P]),
                    wait=merge(tmt2, rb))
                tcs2.append(tc)
                dst = self.PB[5] if which == 0 else self.PB[4]
                mmb = ev("tensor", lambda e, ti=ti, cw=cw, dst=dst: e.matmul(
                    dst[:, ti * P:ti * P + cw], self.ones_r[:],
                    self.qrow[0:1, 0:cw], start=True, stop=True),
                    wait=merge(tc, (self.t5_free if hasattr(self, 't5_free')
                                    else Tok()) if ti == 0 else Tok(),
                               (self.t4_free if hasattr(self, 't4_free')
                                else Tok()) if ti == 0 else Tok()))
                rb = mmb
                (bq if which == 0 else br).append(mmb)
        n4 = min(ncols, 512)
        tbq = ev("vector", lambda e, n=n4: e.tensor_copy(
            self.bc_q[:, 0:n], self.PB[5][:, 0:n]), wait=merge(*bq))
        tbr = ev("vector", lambda e, n=n4: e.tensor_scalar(
            self.bc_r[:, 0:n], self.PB[4][:, 0:n], self.wsc[:, 3:4], None,
            ALU.mult), wait=merge(*br))
        if nti == 5:
            for which in range(2):
                pbsrc = self.PB[3] if which == 0 else self.PB[2]
                srct = self.s2tok if which == 0 else self.rs2tok
                tmx = ev("tensor", lambda e, pbs=pbsrc, st_=srct: e.transpose(
                    pbs[0:1, 0:P], st_[:, 4:5], self.ident_f[:]),
                    wait=merge(tok_s2, rb))
                tc = ev("vector", lambda e, pbs=pbsrc: e.tensor_copy(
                    self.qrow[0:1, 0:P], pbs[0:1, 0:P]), wait=merge(tmx, rb))
                tcs2.append(tc)
                dst = self.PB[5] if which == 0 else self.PB[4]
                mmb = ev("tensor", lambda e, dst=dst: e.matmul(
                    dst[:, 0:1], self.ones_r[:], self.qrow[0:1, 0:1],
                    start=True, stop=True),
                    wait=merge(tc, tbq if which == 0 else tbr))
                rb = mmb
                if which == 0:
                    tbq = merge(tbq, ev("vector", lambda e: e.tensor_copy(
                        self.bc_q[:, 512:513], self.PB[5][:, 0:1]), wait=mmb))
                else:
                    tbr = merge(tbr, ev("vector", lambda e: e.tensor_scalar(
                        self.bc_r[:, 512:513], self.PB[4][:, 0:1],
                        self.wsc[:, 3:4], None, ALU.mult), wait=mmb))
        self.t5_free = tbq
        self.t4_free = tbr
        self.pb3_free = merge(self.pb3_free, *tcs2)
        self.pb2_free = merge(self.pb2_free, *tcs2)
        if self.qlimit < 3:
            return
        hq_toks = []
        qb = merge(tbq)
        for fc in range(NF1):
            tq1 = ev("vector", lambda e, fc=fc, c0=col0, n=ncols:
                     e.tensor_tensor(
                self.acc_c[:, 0:n], self.h_t[:, fc, c0:c0 + n],
                self.bc_q[:, 0:n], ALU.mult),
                wait=merge(qb, self.h_busy[fc]))
            tq2 = ev("vector", lambda e, n=ncols: e.tensor_scalar(
                self.qf16[:, 0:n], self.acc_c[:, 0:n], MAGIC, None, ALU.add),
                wait=merge(tq1, self.qf16_busy))
            tq3 = ev("vector", lambda e, fc=fc, n=ncols: e.tensor_scalar(
                self.hq_t[:, fc, 0:n], self.qf16[:, 0:n],
                -MAGIC, None, ALU.add), wait=merge(tq2, self.hq_busy[fc]))
            self.qf16_busy = tq3
            self.hq_busy[fc] = tq3
            self.h_busy[fc] = tq1
            qb = tq1
            hq_toks.append(tq3)
        self.acc_busy = merge(*hq_toks[-1:])
        tok_hq = merge(*hq_toks)
        if self.qlimit < 4:
            return
        n512 = min(ncols, 512)
        tail = ncols - n512
        for m in range(ND):
            pb = self.PB[m % 2]
            mm = None
            wt = None
            for fb in range(4):
                wsl = 2 + (fb % 2)
                wt = self.st(
                    self.w12_ring[:, wsl, :, :],
                    self.s_w2qT[fb * 8:fb * 8 + 8, m].transpose([1, 0, 2]),
                    wait=merge(self.wprep_tok, self.w12_busy[wsl]),
                    q="gpsimd", dsem=f"dw12_{wsl}")
                for j in range(8):
                    fc = fb * 8 + j
                    mm = ev("tensor", lambda e, fc=fc, j=j, pb=pb, wsl=wsl,
                            n=n512: e.matmul(
                        pb[:, 0:n], self.w12_ring[:, wsl, j, :],
                        self.hq_t[:, fc, 0:n],
                        start=(fc == 0), stop=(fc == NF1 - 1 and True),
                        skip_group_check=True),
                        wait=merge(wt, tok_hq,
                                   self.y_psum_free[m % 2] if fc == 0 else None))
                    if tail:
                        mm = ev("tensor", lambda e, fc=fc, j=j, wsl=wsl: e.matmul(
                            self.PB[2][:, 0:1], self.w12_ring[:, wsl, j, :],
                            self.hq_t[:, fc, 512:513],
                            start=(fc == 0), stop=(fc == NF1 - 1),
                            skip_group_check=True),
                            wait=merge(self.tail_free, self.pb2_free)
                            if fc == 0 else None)
                self.w12_busy[wsl] = mm
            tye = ev("vector", lambda e, pb=pb, n=n512: e.tensor_tensor(
                self.acc_c[:, 0:n], pb[:, 0:n], self.bc_r[:, 0:n],
                ALU.mult), wait=merge(mm, tbr, self.ao2_busy[0]))
            self.y_psum_free[m % 2] = tye
            if tail:
                tyet = ev("vector", lambda e, m=m: e.scalar_tensor_tensor(
                    self.sc_a[:, 8 + m:9 + m], self.PB[2][:, 0:1], 1.0,
                    self.bc_r[:, 512:513], ALU.mult, ALU.mult), wait=merge(mm, tbr))
                self.tail_free = tyet
            if self.qlimit < 5:
                continue
            tcp = None
            if tail:
                tcp = ev("vector", lambda e, m=m: e.tensor_copy(
                    self.acc_c[:, 512:513], self.sc_a[:, 8 + m:9 + m]),
                    wait=tyet)
            j0b = 0 if w == 0 else 1
            tts = []
            for ti in range(4):
                j0 = j0b + ti * P
                cw = min(P, ncols - j0)
                tts.append(ev("tensor", lambda e, ti=ti, j0=j0, cw=cw:
                              e.transpose(
                    self.PB[6][0:cw, ti * P:ti * P + P],
                    self.acc_c[:, j0:j0 + cw], self.ident_f[:]),
                    wait=merge(tye, tcp, self.pb6_free2)))
            tts_all = tts[-1]
            if self.qlimit < 6:
                self.pb6_free2 = tts_all
                self.ao2_busy[0] = tts_all
                continue
            adds = []
            for ti in range(4):
                j0 = j0b + ti * P
                cw = min(P, ncols - j0)
                tadd = ev("vector", lambda e, ti=ti, cw=cw, m=m, w=w:
                          e.tensor_add(
                    self.x1_t[0:cw, 4 * w + ti, m * P:(m + 1) * P],
                    self.PB[6][0:cw, ti * P:(ti + 1) * P],
                    self.x1_t[0:cw, 4 * w + ti, m * P:(m + 1) * P]),
                    wait=merge(tts_all, self.tok_x1))
                adds.append(tadd)
                self.out_ready[4 * w + ti] = merge(
                    self.out_ready.get(4 * w + ti), tadd)
            self.pb6_free2 = merge(*adds)
            self.ao2_busy[0] = tts_all
            if w >= 1:
                ttl = ev("tensor", lambda e: e.transpose(
                    self.PB[3][0:1, 0:P], self.acc_c[:, 0:1],
                    self.ident_f[:]), wait=merge(tye, self.pb3_free))
                tev_t = ev("vector", lambda e: e.tensor_copy(
                    self.qrow[64:65, 0:P], self.PB[3][0:1, 0:P]),
                    wait=merge(ttl, self.qrow_tail_busy))
                self.pb3_free = tev_t
                ldma = self.st(
                    self.x1_t[127:128, 4 * w - 1, m * P:(m + 1) * P],
                    self.qrow[64:65, 0:P], q="gpsimd", dsem=f"dyadd{m % 2}",
                    accum=True, wait=merge(tev_t, self.tok_x1))
                self.qrow_tail_busy = ldma
                self.out_ready[4 * w - 1] = merge(
                    self.out_ready.get(4 * w - 1), ldma)
                self.ao2_busy[0] = merge(self.ao2_busy[0], ttl)
                self.y_dma_last = merge(self.y_dma_last, ldma)

    def build(self):
        self.declare()
        es = contextlib.ExitStack()
        with es:
            self.pr = Prog(self.nc, es)
            for name in ("d_sp", "d_gp", "p_pe", "p_dve", "p_act", "p_gp"):
                self.pr.sem(name)
            block = es.enter_context(self.nc.Block())

            self.emit_setup()
            out_toks = []
            L = self.limit
            for s in range(BLOC):
                self.tok_heads = Tok()
                self.out_ready = {}
                self.tail_free = Tok()
                if L >= 1:
                    self.emit_ln1(s)
                for sg in range(NSG):
                    if L >= 2:
                        self.emit_qkv(s, sg)
                    if L >= 3:
                        self.emit_attention(s, sg)
                if L >= 4:
                    self.emit_wout(s)
                if L >= 5:
                    self.emit_ln2(s)
                for w in range(4):
                    if L >= 6:
                        self.emit_ffn_w1(s, w)
                    if L >= 7:
                        self.emit_conv(s, w)
                    if L >= 8:
                        self.emit_quant2_w2(s, w)
                if L < 3.5:
                    continue
                for tt in range(NT):
                    out_toks.append(self.st(
                        self.out_d[s, tt * P:(tt + 1) * P, :],
                        self.x1_t[:, tt, :],
                        wait=self.out_ready.get(tt, Tok()),
                        dsem=f"dout{tt % 4}"))
            total = merge(*out_toks, getattr(self, "y_dma_last", None))
            self.pr.emit("sync", lambda e: e.nop(), wait=total)
            self.pr.emit("gpsimd", lambda e: e.engine_nop(), wait=total)
            self.pr.run(block)
        return self.nc


def kernel(**inputs):
    x = np.ascontiguousarray(inputs["L0_attn_hidden_states"])
    wqkv = np.ascontiguousarray(inputs["L0_attn_w_qkv"])
    wout = np.ascontiguousarray(inputs["L0_attn_w_out"])
    w1 = np.ascontiguousarray(inputs["L0_ffn_w1"])
    convw = np.ascontiguousarray(
        np.asarray(inputs["L0_ffn_conv_w"]).reshape(NF1, P, 3)
        .transpose(1, 0, 2).reshape(P, NF1 * 3))
    convb = np.ascontiguousarray(
        np.asarray(inputs["L0_ffn_conv_b"]).reshape(NF1, P)
        .transpose(1, 0))
    w2 = np.ascontiguousarray(inputs["L0_ffn_w2"])

    mk = MK()
    nc = mk.build()
    in_maps = [{
        "x": x[c * BLOC:(c + 1) * BLOC],
        "wqkv": wqkv, "wout": wout, "w1": w1,
        "convw": convw, "convb": convb, "w2": w2,
    } for c in range(NCORES)]
    res = run_bass_kernel_spmd(nc, in_maps, core_ids=list(range(NCORES)))
    return np.concatenate([r["out"] for r in res.results], axis=0)


# version 07 (final)
# speedup vs baseline: 1.0238x; 1.0238x over previous
import contextlib
import numpy as np
import ml_dtypes

import concourse.bass as bass
import concourse.mybir as mybir
from concourse.bass_utils import run_bass_kernel_spmd

F32 = mybir.dt.float32
F16 = mybir.dt.float16
BF16 = mybir.dt.bfloat16
I8 = mybir.dt.int8
AF = mybir.ActivationFunctionType
ALU = mybir.AluOpType
AX = mybir.AxisListType

P = 128
NCORES = 8
BLOC = 2
S = 2048
NT = S // P
D = 1024
ND = D // P
F = 3072
NF = F // P
F1 = 4096
NF1 = F1 // P
NSG = 4
SB_TOP = 229376 - 128

MAGIC = 1536.0


class Tok(dict):
    def merged(self, other):
        t = Tok(self)
        for k, v in (other or {}).items():
            t[k] = max(t.get(k, 0), v)
        return t


def row_busy_get(rb, r):
    return rb[0] if r in (0, 64) else rb[1]


def row_busy_set(rb, r, tok):
    if r in (0, 64):
        rb[0] = tok
    else:
        rb[1] = tok


def merge(*toks):
    t = Tok()
    for o in toks:
        if o:
            t = t.merged(o)
    return t


class Prog:
    ENGINES = ("tensor", "vector", "scalar", "gpsimd", "sync")

    def __init__(self, nc, es):
        self.nc = nc
        self.es = es
        self.streams = {k: [] for k in self.ENGINES}
        self.sems = {}
        self.guaranteed = {k: {} for k in self.ENGINES}

    def sem(self, name):
        h = self.es.enter_context(self.nc.semaphore(name))
        self.sems[name] = [h, 0]

    def dsem(self, name):
        if name not in self.sems:
            self.sem(name)
        return name

    OWN = {"vector": "p_dve", "scalar": "p_act", "gpsimd": "p_gp"}

    def emit(self, eng, fn, wait=None, inc=None, inc_amt=1):
        rec = None
        if inc is not None:
            rec = self.sems[inc]
            rec[1] += inc_amt
        wait = dict(wait or {})
        own = self.OWN.get(eng)
        if own is not None:
            prior = self.sems[own][1] - (inc_amt if inc == own else 0)
            if prior > 0:
                wait[own] = max(wait.get(own, 0), prior)
        waits = []
        g = self.guaranteed[eng]
        for name, v in wait.items():
            if g.get(name, -1) < v:
                waits.append((self.sems[name][0], v))
                g[name] = v
        sem_h = rec[0] if rec is not None else None

        def run(e, _fn=fn, _w=waits, _sh=sem_h, _amt=inc_amt):
            for hh, vv in _w:
                e.wait_ge(hh, vv)
            ins = _fn(e)
            if _sh is not None:
                ins.then_inc(_sh, _amt)
            return ins

        self.streams[eng].append(run)
        return Tok({inc: rec[1]}) if inc is not None else None

    def fence_all(self):
        return Tok({k: v[1] for k, v in self.sems.items() if v[1] > 0})

    def run(self, block):
        for name in self.ENGINES:
            stream = self.streams[name]

            def body(e, _s=stream):
                for fn in _s:
                    fn(e)

            getattr(block, name)(body)


def host_constants():
    c = {}
    c["ident_f32"] = np.eye(P, dtype=np.float32)
    c["ident_bf16"] = np.eye(P, dtype=np.float32).astype(ml_dtypes.bfloat16)
    c["ones_row"] = np.ones((1, P), dtype=np.float32)
    pos = np.arange(S, dtype=np.float32)
    inv_freq = (1.0 / 10000.0 ** (np.arange(0, 32, 2, dtype=np.float32) / 32.0)
                ).astype(np.float32)
    sinusoid = np.einsum("i,j->ij", pos, inv_freq).astype(np.float32)
    def tilemaj(t):
        return np.ascontiguousarray(
            t.reshape(NT, P, 16).transpose(1, 0, 2).reshape(P, NT * 16))
    c["sin_t"] = tilemaj(np.sin(sinusoid).astype(np.float32))
    c["cos_t"] = tilemaj(np.cos(sinusoid).astype(np.float32))
    i = np.arange(P)[:, None]
    j = np.arange(512)[None, :]
    masks = [np.where(j <= i + 128 * r, 0.0, -3e38).astype(np.float32)
             for r in range(4)]
    c["masks"] = np.concatenate(masks, axis=1).astype(ml_dtypes.bfloat16)
    return c


class MK:
    def __init__(self, debug=(), limit=99, slimit=99, climit=99, wlimit=99,
                 qlimit=99):
        self.debug = set(debug)
        self.limit = limit
        self.slimit = slimit
        self.climit = climit
        self.wlimit = wlimit
        self.qlimit = qlimit
        self.nc = bass.Bass("TRN2", target_bir_lowering=False)
        self.dbg = {}

    def declare(self):
        nc = self.nc
        self.x_d = nc.dram_tensor("x", [BLOC, S, D], F32, kind="ExternalInput").ap()
        self.wqkv_d = nc.dram_tensor("wqkv", [F, D], F32, kind="ExternalInput").ap()
        self.wout_d = nc.dram_tensor("wout", [D, D], F32, kind="ExternalInput").ap()
        self.w1_d = nc.dram_tensor("w1", [F1, D], F32, kind="ExternalInput").ap()
        self.convw_d = nc.dram_tensor("convw", [P, NF1 * 3], F32,
                                      kind="ExternalInput").ap()
        self.convb_d = nc.dram_tensor("convb", [P, NF1], F32,
                                      kind="ExternalInput").ap()
        self.w2_d = nc.dram_tensor("w2", [D, F1], F32, kind="ExternalInput").ap()
        self.out_d = nc.dram_tensor("out", [BLOC, S, D], F32, kind="ExternalOutput").ap()

        self.s_wqkvT = {p: nc.dram_tensor(f"s_wqkvT_{p}", [ND, NF, P, P], BF16).ap()
                        for p in ("hi", "lo")}
        self.s_woutT = {p: nc.dram_tensor(f"s_woutT_{p}", [ND, ND, P, P], BF16).ap()
                        for p in ("hi", "lo")}
        self.s_w1qT = nc.dram_tensor("s_w1qT", [ND, NF1, P, P], BF16).ap()
        self.s_w2qT = nc.dram_tensor("s_w2qT", [NF1, ND, P, P], BF16).ap()
        self.s_xhat = {p: nc.dram_tensor(f"s_xhat_{p}", [BLOC, ND, NT, P, P], BF16).ap()
                       for p in ("hi", "lo")}

        consts = host_constants()
        self.inl = {k: nc.inline_tensor(v, name=f"c_{k}") for k, v in consts.items()}

        A = lambda name, shape, dtype: nc.alloc_sbuf_tensor(name, shape, dtype).ap()
        self.ident_f = A("ident_f", [P, P], F32)
        self.ident_b = A("ident_b", [P, P], BF16)
        self.ones_r = A("ones_r", [1, P], F32)
        self.cos_s = A("cos_s", [P, NT * 16], F32)
        self.sin_s = A("sin_s", [P, NT * 16], F32)
        self.mask_s = A("mask_s", [P, 2048], BF16)
        self.convw_s = A("convw_s", [P, NF1 * 3], F32)
        self.convb_s = A("convb_s", [P, NF1], F32)
        self.wsc = A("wsc", [P, 8], F32)
        self.ln_m = A("ln_m", [P, NT], F32)
        self.ln_r = A("ln_r", [P, NT], F32)
        self.ln2_m = A("ln2_m", [P, NT], F32)
        self.ln2_r = A("ln2_r", [P, NT], F32)
        self.mx_acc = A("mx_acc", [P, 8], F32)
        self.m0_bc = A("m0_bc", [P, 4], F32)
        self.qn2 = A("qn2", [P, NT * 4], F32)
        self.mbias = A("mbias", [P, NT * 4], F32)
        self.sc_a = A("sc_a", [P, 16], F32)
        self.eps_c = A("eps_c", [P, 1], F32)
        self.sc_b = A("sc_b", [P, 16], F32)
        self.qrow = A("qrow", [P, P], F32)
        self.sums_c = A("sums_c", [P, 16], F32)
        self.recip_c = A("recip_c", [P, 4], F32)
        self.s1tok = A("s1tok", [P, NT], F32)
        self.rs1tok = A("rs1tok", [P, NT], F32)
        self.s2tok = A("s2tok", [P, 8], F32)
        self.rs2tok = A("rs2tok", [P, 8], F32)
        self.prev2 = A("prev2", [P, NF1 * 2], F32)

        off = [SB_TOP]

        def arena(name, shape, dtype, align=128):
            nbytes = int(np.prod(shape[1:])) * mybir.dt.size(dtype)
            nbytes = (nbytes + align - 1) // align * align
            off[0] -= nbytes
            return nc.alloc_sbuf_tensor_at(name, shape, dtype, offset=off[0]).ap()

        def at(name, shape, dtype, o):
            return nc.alloc_sbuf_tensor_at(name, shape, dtype, offset=o).ap()

        self.x1_t = arena("x1_t", [P, NT, D], F32)
        self.h_t = arena("h_t", [P, NF1, 516], F32)
        hbase = off[0]
        self.heads_t = at("heads_t", [P, NT, D], F32, hbase)
        self.su_w = at("su_w", [P, F1], F32, hbase)
        self.su_q = at("su_q", [P, F1], F16, hbase + F1 * 4)
        self.su_b = at("su_b", [P, F1], BF16, hbase + F1 * 4 + F1 * 2)
        self.su_e = at("su_e", [P, F1], F32, hbase + F1 * 8)
        self.wq_stage = arena("wq_stage", [P, 2, 2, 768], BF16)
        self.kq_rot = arena("kq_rot", [P, 2, 512], F32)
        self.rope_tmp = arena("rope_tmp", [P, 2, 128], F32)
        self.xst_ring = arena("xst_ring", [P, 8, P], BF16)
        self.v_tok = arena("v_tok", [P, NT, 256], F32)
        self.kqf = arena("kqf", [P, 4, S], F32)
        self.attn_sb = arena("attn_sb", [P, 2, 512], F32)
        self.attnT_sb = arena("attnT_sb", [P, 2, 512], F32)
        wlow = off[0]
        self.wlow = wlow
        o = wlow
        self.ln_x = at("ln_x", [P, 2, D], F32, o); o += 2 * D * 4
        self.ln_xh = at("ln_xh", [P, 2, D], F32, o); o += 2 * D * 4
        self.ln_hi = at("ln_hi", [P, 2, D], BF16, o); o += 2 * D * 2
        self.ln_lo = at("ln_lo", [P, 2, D], BF16, o); o += 2 * D * 2
        self.ln_st = at("ln_st", [P, 2, 1024], BF16, o); o += 2 * 1024 * 2
        self.su_ev = at("su_ev", [P, 2, 1024], BF16, wlow)
        o = wlow
        self.hfw = at("hfw", [P, 2, ND, 512], BF16, o); o += 2 * ND * 512 * 2
        self.wo_stage = at("wo_stage", [P, 6, 2, P], BF16, o); o += 6 * 2 * P * 2
        self.ao_ev = at("ao_ev", [P, 2, 512], F32, o); o += 2 * 512 * 4
        self.xres = at("xres", [P, 4, D], F32, o); o += 4 * D * 4
        o = wlow
        self.hq_t = at("hq_t", [P, NF1, 514], BF16, o); o += NF1 * 514 * 2
        self.xq_win = at("xq_win", [P, ND, 512], BF16, o); o += ND * 512 * 2
        self.x1h_ring = at("x1h_ring", [P, 1, D], F32, o); o += 1 * D * 4
        self.w12_ring = at("w12_ring", [P, 4, 8, P], BF16, o); o += 4 * 8 * P * 2
        self.xqb_st = at("xqb_st", [P, D], BF16, o); o += D * 2
        self.yw = at("yw", [P, 512], F32, o); o += 512 * 4
        self.bc_q = at("bc_q", [P, 544], F32, o); o += 544 * 4
        self.bc_r = at("bc_r", [P, 544], F32, o); o += 544 * 4
        self.acc_c = at("acc_c", [P, 544], F32, o); o += 544 * 4
        self.acc2 = at("acc2", [P, 544], F32, o); o += 544 * 4
        self.qf16 = at("qf16", [P, 1024], F16, o); o += 1024 * 2
        self.bc_w1 = self.bc_q
        ffn_chain_top = o

        self.arena_low = off[0]
        assert ffn_chain_top <= hbase, (ffn_chain_top, hbase)
        used = 212863 - nc.sbuf_bytes_remaining
        statics_top = 16512 + used
        assert statics_top <= off[0], (statics_top, off[0])

        self.PB = [nc.alloc_psum_tensor(f"pb{i}", [P, 512], F32).ap()
                   for i in range(7)]
        self.PBB = nc.alloc_psum_tensor("pbb", [P, 1024], BF16).ap()

    def dbg_out(self, name, shape, dtype=F32):
        if name in self.debug:
            ap = self.nc.dram_tensor(f"dbg_{name}", shape, dtype,
                                     kind="ExternalOutput").ap()
            self.dbg[name] = ap
            return ap
        return None

    def st(self, dst, src, wait=None, q="sync", dsem="d_sp", accum=False):
        sem = self.pr.dsem(dsem)
        prev = self.pr.sems[sem][1]
        wait = dict(wait or {})
        if prev > 0:
            wait[sem] = max(wait.get(sem, 0), prev)
        if accum:
            fn = lambda e, d=dst, s=src: e.dma_start(out=d, in_=s,
                                                     accum_op=ALU.add)
        else:
            fn = lambda e, d=dst, s=src: e.dma_start(out=d, in_=s)
        return self.pr.emit(q, fn, wait=wait, inc=sem, inc_amt=16)

    def ld(self, dst, src, wait=None, q="sync", dsem="d_sp"):
        return self.st(dst, src, wait=wait, q=q, dsem=dsem)

    def ev(self, eng, fn, wait=None):
        sem = {"tensor": "p_pe", "vector": "p_dve", "scalar": "p_act",
               "gpsimd": "p_gp"}[eng]
        return self.pr.emit(eng, fn, wait=wait, inc=sem)

    def emit_consts(self):
        ld = lambda d, s, **kw: self.ld(d, s, dsem="dc", **kw)
        toks = [self.ev("vector", lambda e: e.memset(self.eps_c[:], 1e-5))]
        loads = [
            (self.ident_f[:], self.inl["ident_f32"].ap()),
            (self.ident_b[:], self.inl["ident_bf16"].ap()),
            (self.ones_r[:], self.inl["ones_row"].ap()),
            (self.cos_s[:], self.inl["cos_t"].ap()),
            (self.sin_s[:], self.inl["sin_t"].ap()),
            (self.mask_s[:], self.inl["masks"].ap()),
            (self.convw_s[:], self.convw_d),
            (self.convb_s[:], self.convb_d),
        ]
        for i, (d, s_) in enumerate(loads):
            if i < self.climit:
                toks.append(ld(d, s_))
        self.tok_consts = merge(*toks)

    def emit_wprep_quant(self, w_d, n_tiles, tile_cols, s_col, dst, dst_f_outer):
        ev, ld = self.ev, self.ld
        NC = tile_cols // P
        tok_z = ev("vector", lambda e: e.memset(self.sc_a[:, 0:1], 0.0))
        tk = tok_z
        for i in range(n_tiles):
            tl = ld(self.su_w[:, 0:tile_cols], w_d[i * P:(i + 1) * P, :],
                    wait=merge(tk, self.su_w_dma), dsem="dsuw")
            self.su_w_dma = tl
            tk = ev("vector", lambda e: e.reduce_sum(
                self.sc_a[:, 1:2], self.su_w[:, 0:tile_cols], axis=AX.X,
                apply_absolute_value=True), wait=merge(tl, tok_z))
            tk = ev("vector", lambda e: e.tensor_add(
                self.sc_a[:, 0:1], self.sc_a[:, 0:1], self.sc_a[:, 1:2]))
        self.last_q3 = merge(self.last_q3, tk)
        tk_t = ev("tensor", lambda e: e.transpose(
            self.PB[0][0:1, 0:P], self.sc_a[:, 0:1], self.ident_f[:]),
            wait=merge(tk, self.tok_consts, self.pb_free[0]))
        tk = ev("vector", lambda e: e.reduce_sum(
            self.sc_b[0:1, 0:1], self.PB[0][0:1, 0:P], axis=AX.X), wait=tk_t)
        n_el = n_tiles * P * tile_cols
        tk = ev("vector", lambda e: e.tensor_scalar(
            self.sc_b[0:1, 1:2], self.sc_b[0:1, 0:1], 1.0 / n_el, 1e-5,
            ALU.mult, ALU.max), wait=tk)
        tk = ev("vector", lambda e: e.reciprocal(
            self.sc_b[0:1, 2:3], self.sc_b[0:1, 1:2]), wait=tk)
        tk_b = ev("tensor", lambda e: e.matmul(
            self.PB[1][:, 0:2], self.ones_r[:], self.sc_b[0:1, 1:3],
            start=True, stop=True), wait=merge(tk, self.pb_free[1]))
        tok_sc = ev("vector", lambda e: e.tensor_copy(
            self.wsc[:, s_col + 1:s_col + 2], self.PB[1][:, 0:1]), wait=tk_b)
        tok_sc = merge(tok_sc, ev("vector", lambda e: e.tensor_copy(
            self.wsc[:, s_col:s_col + 1], self.PB[1][:, 1:2]), wait=tk_b))
        self.pb_free[0] = tk
        self.pb_free[1] = tok_sc
        ev_busy = self.su_ev_busy0
        for i in range(n_tiles):
            tl = ld(self.su_w[:, 0:tile_cols], w_d[i * P:(i + 1) * P, :],
                    wait=merge(self.last_q3, self.su_w_dma), dsem="dsuw")
            self.su_w_dma = tl
            q1 = ev("vector", lambda e: e.tensor_scalar(
                self.su_q[:, 0:tile_cols], self.su_w[:, 0:tile_cols],
                self.wsc[:, s_col:s_col + 1], MAGIC, ALU.mult, ALU.add),
                wait=merge(tl, tok_sc))
            q2 = ev("vector", lambda e: e.tensor_scalar(
                self.su_b[:, 0:tile_cols], self.su_q[:, 0:tile_cols],
                -MAGIC, None, ALU.add), wait=q1)
            q2 = ev("vector", lambda e: e.scalar_tensor_tensor(
                self.su_e[:, 0:tile_cols], self.su_w[:, 0:tile_cols],
                self.wsc[:, s_col:s_col + 1], self.su_b[:, 0:tile_cols],
                ALU.mult, ALU.subtract), wait=q2)
            q2 = ev("vector", lambda e: e.tensor_scalar(
                self.su_q[:, 0:tile_cols], self.su_e[:, 0:tile_cols],
                0.5, None, ALU.is_gt), wait=q2)
            q2 = ev("vector", lambda e: e.tensor_add(
                self.su_b[:, 0:tile_cols], self.su_b[:, 0:tile_cols],
                self.su_q[:, 0:tile_cols]), wait=q2)
            q2 = ev("vector", lambda e: e.tensor_scalar(
                self.su_q[:, 0:tile_cols], self.su_e[:, 0:tile_cols],
                -0.5, None, ALU.is_lt), wait=q2)
            q2 = ev("vector", lambda e: e.tensor_sub(
                self.su_b[:, 0:tile_cols], self.su_b[:, 0:tile_cols],
                self.su_q[:, 0:tile_cols]), wait=q2)
            q3 = ev("vector", lambda e: e.tensor_scalar(
                self.su_b[:, 0:tile_cols], self.su_b[:, 0:tile_cols],
                -1.0, 1.0, ALU.max, ALU.min), wait=q2)
            self.last_q3 = q3
            for b0 in range(0, NC, 8):
                nb = min(8, NC - b0)
                tts = []
                for j in range(nb):
                    tts.append(ev("tensor", lambda e, jj=b0 + j, j2=j: e.transpose(
                        self.PBB[:, j2 * P:(j2 + 1) * P],
                        self.su_b[:, jj * P:(jj + 1) * P], self.ident_b[:]),
                        wait=merge(q3, self.pbb_free)))
                tev = ev("vector", lambda e, n=nb: e.tensor_copy(
                    self.su_ev[:, 0, 0:n * P], self.PBB[:, 0:n * P]),
                    wait=merge(tts[-1], ev_busy))
                self.pbb_free = tev
                dap = dst[b0:b0 + nb, i]
                src = self.su_ev[:, 0, 0:nb * P].rearrange("p (n c) -> p n c", n=nb)
                ev_busy = self.st(dap.transpose([1, 0, 2]), src,
                                  wait=tev, q="sync", dsem="dwst0")
                self.su_ev_busy0 = ev_busy
                self.wprep_tok = merge(self.wprep_tok, ev_busy)

    def emit_wprep_split(self, w_d, n_rows_tiles, dst_hi, dst_lo):
        ev, ld = self.ev, self.ld
        ev_busy = self.su_ev_busy1
        for i in range(n_rows_tiles):
            tl = ld(self.su_w[:, 0:D], w_d[i * P:(i + 1) * P, :],
                    wait=merge(self.last_q3, self.su_w_dma), dsem="dsuw")
            self.su_w_dma = tl
            h1 = ev("vector", lambda e: e.tensor_copy(
                self.su_b[:, 0:D], self.su_w[:, 0:D]),
                wait=merge(tl, self.tok_consts))
            l1 = ev("vector", lambda e: e.tensor_sub(
                self.su_b[:, D:2 * D], self.su_w[:, 0:D], self.su_b[:, 0:D]),
                wait=h1)
            self.last_q3 = l1
            for pol, src_off, dst_t in (("hi", 0, dst_hi), ("lo", D, dst_lo)):
                tts = []
                for j in range(ND):
                    tts.append(ev("tensor", lambda e, jj=j, so=src_off: e.transpose(
                        self.PBB[:, jj * P:(jj + 1) * P],
                        self.su_b[:, so + jj * P:so + (jj + 1) * P],
                        self.ident_b[:]), wait=merge(l1, self.pbb_free)))
                tev = ev("vector", lambda e: e.tensor_copy(
                    self.su_ev[:, 1, 0:ND * P], self.PBB[:, 0:ND * P]),
                    wait=merge(tts[-1], ev_busy))
                self.pbb_free = tev
                dap = dst_t[:, i]
                src = self.su_ev[:, 1, 0:ND * P].rearrange("p (n c) -> p n c", n=ND)
                ev_busy = self.st(dap.transpose([1, 0, 2]), src,
                                  wait=tev, q="gpsimd", dsem="dwst1")
                self.su_ev_busy1 = ev_busy
                self.wprep_tok = merge(self.wprep_tok, ev_busy)

    def emit_setup(self):
        self.wprep_tok = Tok()
        self.su_w_dma = Tok()
        self.last_q3 = Tok()
        self.su_ev_busy0 = Tok()
        self.su_ev_busy1 = Tok()
        self.pb_free = [Tok() for _ in range(7)]
        self.pbb_free = Tok()
        self.emit_consts()
        if self.slimit >= 1:
            self.emit_wprep_quant(self.w1_d, NF1, D, 0, self.s_w1qT, False)
        if self.slimit >= 2:
            self.emit_wprep_quant(self.w2_d, ND, F1, 2, self.s_w2qT, True)
        if self.slimit >= 3:
            self.emit_wprep_split(self.wqkv_d, NF, self.s_wqkvT["hi"],
                                  self.s_wqkvT["lo"])
        if self.slimit >= 4:
            self.emit_wprep_split(self.wout_d, ND, self.s_woutT["hi"],
                                  self.s_woutT["lo"])

    def emit_ln1(self, s):
        ev, ld = self.ev, self.ld
        eps = 1e-5
        fence = self.pr.fence_all()
        out_toks = []
        busy = [fence, fence]
        pbb_free = self.pr.fence_all()
        for tt in range(NT):
            sl = tt % 2
            tl = ld(self.ln_x[:, sl, :], self.x_d[s, tt * P:(tt + 1) * P, :],
                    wait=busy[sl], dsem=f"dlnx{sl}")
            t_sum = ev("vector", lambda e, sl=sl: e.reduce_sum(
                self.sc_a[:, 0:1], self.ln_x[:, sl, :], axis=AX.X), wait=tl)
            t_sq = ev("scalar", lambda e, sl=sl: e.activation(
                self.ln_xh[:, sl, :], self.ln_x[:, sl, :], AF.Square,
                accum_out=self.sc_a[:, 1:2]), wait=tl)
            tk = ev("vector", lambda e: e.tensor_scalar(
                self.sc_a[:, 2:3], self.sc_a[:, 0:1], 1.0 / D, None, ALU.mult),
                wait=merge(t_sum, t_sq))
            tk = ev("vector", lambda e: e.tensor_tensor(
                self.sc_a[:, 3:4], self.sc_a[:, 2:3], self.sc_a[:, 2:3],
                ALU.mult), wait=tk)
            tk = ev("vector", lambda e: e.scalar_tensor_tensor(
                self.sc_a[:, 4:5], self.sc_a[:, 1:2], 1.0 / D, self.sc_a[:, 3:4],
                ALU.mult, ALU.subtract), wait=tk)
            tk = ev("scalar", lambda e: e.activation(
                self.sc_a[:, 5:6], self.sc_a[:, 4:5], AF.Sqrt,
                bias=self.eps_c[:, 0:1], scale=1.0), wait=tk)
            tk = ev("vector", lambda e, tt=tt: e.reciprocal(
                self.ln_r[:, tt:tt + 1], self.sc_a[:, 5:6]), wait=tk)
            tk = ev("vector", lambda e, tt=tt: e.scalar_tensor_tensor(
                self.ln_m[:, tt:tt + 1], self.sc_a[:, 2:3], -1.0,
                self.ln_r[:, tt:tt + 1], ALU.mult, ALU.mult), wait=tk)
            t_xh = ev("scalar", lambda e, sl=sl, tt=tt: e.activation(
                self.ln_xh[:, sl, :], self.ln_x[:, sl, :], AF.Identity,
                bias=self.ln_m[:, tt:tt + 1], scale=self.ln_r[:, tt:tt + 1]),
                wait=tk)
            t_hi = ev("vector", lambda e, sl=sl: e.tensor_copy(
                self.ln_hi[:, sl, :], self.ln_xh[:, sl, :]), wait=t_xh)
            t_lo = ev("vector", lambda e, sl=sl: e.tensor_sub(
                self.ln_lo[:, sl, :], self.ln_xh[:, sl, :], self.ln_hi[:, sl, :]),
                wait=t_hi)
            stok = None
            for pol, src, tok in (("hi", self.ln_hi, t_hi), ("lo", self.ln_lo, t_lo)):
                tts = []
                for j in range(ND):
                    tts.append(ev("tensor", lambda e, sl=sl, jj=j, sr=src: e.transpose(
                        self.PBB[:, jj * P:(jj + 1) * P],
                        sr[:, sl, jj * P:(jj + 1) * P], self.ident_b[:]),
                        wait=merge(tok, pbb_free)))
                tev = ev("vector", lambda e, sl=sl: e.tensor_copy(
                    self.ln_st[:, sl, :], self.PBB[:, 0:ND * P]),
                    wait=merge(tts[-1], stok))
                pbb_free = tev
                dap = self.s_xhat[pol][s, :, tt]
                srcap = self.ln_st[:, sl, :].rearrange("p (n c) -> p n c", n=ND)
                stok = self.st(dap.transpose([1, 0, 2]), srcap,
                               wait=tev, q="gpsimd", dsem="dxh")
                out_toks.append(stok)
            busy[sl] = merge(t_sum, t_sq, t_xh, stok)
        self.xhat_tok = merge(*out_toks)
        if s == 0:
            d = self.dbg_out("xhat_hi0", [ND, NT, P, P], BF16)
            if d is not None:
                self.st(d, self.s_xhat["hi"][0], wait=self.xhat_tok, dsem="ddbg")
            d = self.dbg_out("xhat_lo0", [ND, NT, P, P], BF16)
            if d is not None:
                self.st(d, self.s_xhat["lo"][0], wait=self.xhat_tok, dsem="ddbg")

    def emit_qkv(self, s, sg):
        ev, ld = self.ev, self.ld
        fence = self.pr.fence_all()
        fk, fq, fv = 6 * sg + 4, 6 * sg + 0, 6 * sg + 2

        mx_z = ev("vector", lambda e: e.memset(self.mx_acc[:], 0.0), wait=fence)
        self.attnsb_free = fence
        xst_busy = [fence] * 8
        wstage_busy = [fence, fence]
        ring = 0
        kqf_done = []
        v_done = []
        kq_free = [fence] * 4
        v_free = [fence, fence]
        pb6_free = fence
        rot_busy = [fence] * 2

        for tb in range(4):
            ts_ = [4 * tb + i for i in range(4)]
            mm_last = {}
            for d in range(ND):
                wsl = d % 2
                wtoks = []
                for pol_i, pol in enumerate(("hi", "lo")):
                    src_t = self.s_wqkvT[pol]
                    for kk, f0 in ((0, fk), (256, fq), (512, fv)):
                        wtoks.append(self.st(
                            self.wq_stage[:, wsl, pol_i, kk:kk + 256],
                            src_t[d, f0:f0 + 2].transpose([1, 0, 2]),
                            wait=merge(self.wprep_tok, wstage_busy[wsl]),
                            q="sync", dsem=f"dwq{wsl}"))
                tok_w = merge(*wtoks)
                tv = None
                for ti, t in enumerate(ts_):
                    hi_slot = (ring * 2) % 8
                    lo_slot = (ring * 2 + 1) % 8
                    ring += 1
                    x_hi = self.st(self.xst_ring[:, hi_slot, :],
                                   self.s_xhat["hi"][s, d, t],
                                   wait=merge(self.xhat_tok, xst_busy[hi_slot]),
                                   q="sync", dsem=f"dxs{hi_slot}")
                    x_lo = self.st(self.xst_ring[:, lo_slot, :],
                                   self.s_xhat["lo"][s, d, t],
                                   wait=merge(self.xhat_tok, xst_busy[lo_slot]),
                                   q="sync", dsem=f"dxs{lo_slot}")
                    first = d == 0
                    last = d == ND - 1
                    pkq = self.PB[ti]
                    pv = self.PB[4 + ti // 2][:, (ti % 2) * 256:(ti % 2) * 256 + 256]
                    wt = merge(tok_w, x_hi, x_lo)
                    if first:
                        wt = merge(wt, kq_free[ti], v_free[ti // 2])
                    passes = [(hi_slot, 0), (hi_slot, 1), (lo_slot, 0)]
                    tkq = None
                    for pi, (xslot, wpol) in enumerate(passes):
                        stf = first and pi == 0
                        stf_v = stf and (ti % 2 == 0)
                        spf = last and pi == 2
                        tkq = ev("tensor", lambda e, xs_=xslot, wp=wpol, sf=stf,
                                 ef=spf, pk=pkq, ws=wsl: e.matmul(
                            pk[:, 0:512], self.xst_ring[:, xs_, :],
                            self.wq_stage[:, ws, wp, 0:512],
                            start=sf, stop=ef, skip_group_check=True), wait=wt)
                        tv = ev("tensor", lambda e, xs_=xslot, wp=wpol, sf=stf_v,
                                ef=spf, pv_=pv, ws=wsl: e.matmul(
                            pv_, self.xst_ring[:, xs_, :],
                            self.wq_stage[:, ws, wp, 512:768],
                            start=sf, stop=ef, skip_group_check=True))
                        wt = None
                    xst_busy[hi_slot] = tv
                    xst_busy[lo_slot] = tv
                    if last:
                        mm_last[t] = (ti, tkq, tv)
                wstage_busy[wsl] = tv
            for t, (ti, tkq, tv) in mm_last.items():
                sl = t % 2
                pkq = self.PB[ti]
                pv = self.PB[4 + ti // 2][:, (ti % 2) * 256:(ti % 2) * 256 + 256]
                partner = ti ^ 1
                tv_partner = mm_last[[tt for tt, (tj, _, _) in mm_last.items()
                                      if tj == partner][0]][2]
                t_ve = ev("vector", lambda e, t=t, pv_=pv: e.tensor_copy(
                    self.v_tok[:, t, :], pv_),
                    wait=merge(tv, tkq, tv_partner))
                v_free[ti // 2] = merge(v_free[ti // 2], t_ve)
                v_done.append(t_ve)
                pk3 = pkq[:, 0:512].rearrange("p (g c) -> p g c", g=8)
                rot3 = self.kq_rot[:, sl, :].rearrange("p (g c) -> p g c", g=8)
                ct = self.cos_s[:, t * 16:(t + 1) * 16]
                st_ = self.sin_s[:, t * 16:(t + 1) * 16]
                cb = ct.rearrange("p (o f) -> p o f", o=1).broadcast_to([P, 8, 16])
                sb = st_.rearrange("p (o f) -> p o f", o=1).broadcast_to([P, 8, 16])
                tmp_e = self.rope_tmp[:, 0, :].rearrange("p (g c) -> p g c", g=8)
                tmp_o = self.rope_tmp[:, 1, :].rearrange("p (g c) -> p g c", g=8)
                t1 = ev("vector", lambda e, a=pk3, b=cb, o=tmp_e: e.tensor_tensor(
                    o, a[:, :, 0:32:2], b, ALU.mult),
                    wait=merge(tkq, tv, self.tok_consts))
                t2 = ev("vector", lambda e, a=pk3, b=sb, o=tmp_o: e.tensor_tensor(
                    o, a[:, :, 1:32:2], b, ALU.mult))
                t3 = ev("vector", lambda e, a=tmp_e, b=tmp_o, o=rot3, sl=sl:
                        e.tensor_tensor(o[:, :, 0:32:2], a, b, ALU.subtract),
                        wait=rot_busy[sl])
                t4 = ev("vector", lambda e, a=pk3, b=cb, o=tmp_e: e.tensor_tensor(
                    o, a[:, :, 1:32:2], b, ALU.mult))
                t5 = ev("vector", lambda e, a=pk3, b=sb, o=tmp_o: e.tensor_tensor(
                    o, a[:, :, 0:32:2], b, ALU.mult))
                t6 = ev("vector", lambda e, a=tmp_e, b=tmp_o, o=rot3:
                        e.tensor_tensor(o[:, :, 1:32:2], a, b, ALU.add))
                t7 = ev("vector", lambda e, a=pk3, o=rot3: e.tensor_copy(
                    o[:, :, 32:64], a[:, :, 32:64]))
                kq_free[ti] = t7
                sqv = self.attn_sb[:, 0, :].rearrange("p (g c) -> p g c", g=8)
                t8a = ev("vector", lambda e, o=rot3, sq=sqv: e.tensor_tensor(
                    sq, o, o, ALU.mult), wait=merge(t6, t7, self.attnsb_free))
                t8b = ev("vector", lambda e, sq=sqv: e.reduce_sum(
                    self.sc_b[:, 0:8], sq, axis=AX.X), wait=t8a)
                self.attnsb_free = t8b
                t9 = ev("vector", lambda e: e.tensor_tensor(
                    self.mx_acc[:, 0:4], self.mx_acc[:, 0:4], self.sc_b[:, 0:4],
                    ALU.max), wait=merge(t8b, mx_z))
                t9b = ev("vector", lambda e, t=t: e.tensor_copy(
                    self.qn2[:, t * 4:(t + 1) * 4], self.sc_b[:, 4:8]),
                    wait=t8b)
                tts = []
                for c in range(4):
                    tts.append(ev("tensor", lambda e, sl=sl, c=c: e.transpose(
                        self.PB[6][:, c * P:(c + 1) * P],
                        self.kq_rot[:, sl, c * P:(c + 1) * P],
                        self.ident_f[:]), wait=merge(t7, pb6_free)))
                tfe = ev("vector", lambda e, t=t: e.tensor_copy(
                    self.kqf[:, :, t * P:(t + 1) * P],
                    self.PB[6][:, 0:512].rearrange("p (n c) -> p n c", n=4)),
                    wait=tts[-1])
                pb6_free = tfe
                rot_busy[sl] = tfe
                kqf_done.append(tfe)
        tm = ev("tensor", lambda e: e.transpose(
            self.PB[0][0:4, 0:P], self.mx_acc[:, 0:4], self.ident_f[:]),
            wait=merge(merge(*kqf_done), kq_free[0]))
        tr = ev("vector", lambda e: e.reduce_max(
            self.sc_b[0:4, 0:1], self.PB[0][0:4, 0:P], axis=AX.X), wait=tm)
        tm2 = ev("tensor", lambda e: e.transpose(
            self.PB[1][0:1, 0:4], self.sc_b[0:4, 0:1], self.ident_f[0:4, 0:4]),
            wait=merge(tr, kq_free[1]))
        tev = ev("vector", lambda e: e.tensor_copy(
            self.qrow[0:1, 0:4], self.PB[1][0:1, 0:4]), wait=tm2)
        tb2 = ev("tensor", lambda e: e.matmul(
            self.PB[1][:, 4:8], self.ones_r[:], self.qrow[0:1, 0:4],
            start=True, stop=True), wait=tev)
        tok_mk2 = ev("vector", lambda e: e.tensor_copy(
            self.m0_bc[:], self.PB[1][:, 4:8]), wait=tb2)
        q3 = self.qn2[:].rearrange("p (t h) -> p t h", t=NT)
        m3 = self.m0_bc[:].rearrange("p (o h) -> p o h", o=1).broadcast_to(
            [P, NT, 4])
        b3 = self.mbias[:].rearrange("p (t h) -> p t h", t=NT)
        tok_b = ev("vector", lambda e: e.tensor_tensor(
            b3, q3, m3, ALU.mult), wait=merge(tok_mk2, t9b))
        tok_b = ev("scalar", lambda e: e.activation(
            self.mbias[:], self.mbias[:], AF.Sqrt), wait=tok_b)
        tok_b = ev("vector", lambda e: e.tensor_scalar(
            self.mbias[:], self.mbias[:], -0.125, None, ALU.mult), wait=tok_b)
        self.tok_m0 = tok_b
        self.tok_kqf = merge(merge(*kqf_done), merge(*v_done), self.tok_m0)
        if s == 0 and sg == 0:
            d = self.dbg_out("kqf0", [P, 4, S])
            if d is not None:
                self.st(d, self.kqf[:], wait=self.tok_kqf, dsem="ddbg")
            d = self.dbg_out("vtok0", [P, NT, 256])
            if d is not None:
                self.st(d, self.v_tok[:], wait=self.tok_kqf, dsem="ddbg")
            d = self.dbg_out("m00", [P, 4])
            if d is not None:
                self.st(d, self.m0_bc[:], wait=self.tok_kqf, dsem="ddbg")

    def emit_attention(self, s, sg):
        ev = self.ev
        fence = self.pr.fence_all()
        heads_toks = []
        sums_z = ev("vector", lambda e: e.memset(self.sums_c[:], 0.0),
                    wait=fence)
        s_free = [fence] * 4
        t_free = [fence, fence]
        av_free = fence
        asb_busy = [fence] * 2
        atsb_busy = [fence] * 2
        for pair in range(2):
            for qc in range(NT):
                nkb = qc // 4 + 1
                exp_toks = {0: [], 1: []}
                av_last = None
                slot_i = 0
                for kb in range(nkb):
                    klen = min(512, (qc + 1) * P - kb * 512)
                    nch = (klen + P - 1) // P
                    for hh in range(2):
                        hl = 2 * pair + hh
                        sbank_i = 2 * hh + (kb % 2)
                        sbank = self.PB[sbank_i]
                        tq = ev("tensor", lambda e, hh=hh, pair=pair, qc=qc,
                                kb=kb, kl=klen, sb=sbank: e.matmul(
                            sb[:, 0:kl],
                            self.kqf[64 * hh:64 * hh + 64, 2 + pair,
                                     qc * P:(qc + 1) * P],
                            self.kqf[64 * hh:64 * hh + 64, pair,
                                     kb * 512:kb * 512 + kl],
                            start=True, stop=True,
                            tile_position=(64 * hh, 0), skip_group_check=True),
                            wait=merge(self.tok_kqf, s_free[sbank_i]))
                        asl = slot_i % 2
                        col = hh * 8 + kb
                        tq2 = tq
                        if kb == qc // 4:
                            r = qc - 4 * kb
                            tq2 = ev("vector", lambda e, sb=sbank, r=r, kl=klen:
                                     e.tensor_add(
                                sb[:, 0:kl], sb[:, 0:kl],
                                self.mask_s[:, 512 * r:512 * r + kl]),
                                wait=merge(tq, self.tok_consts))
                        texp = ev("scalar", lambda e, sb=sbank, asl=asl, hl=hl,
                                  col=col, kl=klen: e.activation(
                            self.attn_sb[:, asl, 0:kl], sb[:, 0:kl], AF.Exp,
                            bias=self.mbias[:, qc * 4 + hl:qc * 4 + hl + 1],
                            scale=0.125,
                            accum_out=self.sums_c[:, col:col + 1]),
                            wait=merge(tq2, sums_z, asb_busy[asl]))
                        s_free[sbank_i] = texp
                        exp_toks[hh].append(texp)
                        tmask = texp
                        tbank = self.PB[4 + hh]
                        tts = []
                        for c in range(nch):
                            tts.append(ev("tensor", lambda e, asl=asl, c=c,
                                          tb=tbank: e.transpose(
                                tb[:, c * P:(c + 1) * P],
                                self.attn_sb[:, asl, c * P:(c + 1) * P],
                                self.ident_f[:]),
                                wait=merge(tmask, t_free[hh])))
                        atsl = slot_i % 2
                        tev = ev("vector", lambda e, atsl=atsl, tb=tbank, kl=klen:
                                 e.tensor_copy(
                            self.attnT_sb[:, atsl, 0:kl], tb[:, 0:kl]),
                            wait=merge(tts[-1], atsb_busy[atsl]))
                        t_free[hh] = tev
                        asb_busy[asl] = tev
                        for c in range(nch):
                            stf = (kb == 0 and c == 0 and hh == 0)
                            spf = (kb == nkb - 1 and c == nch - 1)
                            av_last = ev("tensor", lambda e, atsl=atsl, c=c,
                                         hh=hh, hl=hl, kb=kb, stf=stf, spf=spf:
                                         e.matmul(
                                self.PB[6][:, 64 * hh:64 * hh + 64],
                                self.attnT_sb[:, atsl, c * P:(c + 1) * P],
                                self.v_tok[:, kb * 4 + c, 64 * hl:64 * hl + 64],
                                start=stf, stop=spf, skip_group_check=True),
                                wait=merge(tev, av_free if stf and hh == 0
                                           and kb == 0 else None))
                            atsb_busy[atsl] = av_last
                        slot_i += 1
                hev = []
                for hh in range(2):
                    hl = 2 * pair + hh
                    hg = 4 * sg + hl
                    tsum = ev("vector", lambda e, hh=hh, nkb=nkb: e.reduce_sum(
                        self.recip_c[:, 2 + hh:3 + hh],
                        self.sums_c[:, hh * 8:hh * 8 + nkb], axis=AX.X),
                        wait=merge(*exp_toks[hh]))
                    trc = ev("vector", lambda e, hh=hh: e.reciprocal(
                        self.recip_c[:, hh:hh + 1],
                        self.recip_c[:, 2 + hh:3 + hh]), wait=tsum)
                    hev.append(ev("vector", lambda e, qc=qc, hg=hg, hh=hh:
                                  e.tensor_scalar(
                        self.heads_t[:, qc, 64 * hg:64 * hg + 64],
                        self.PB[6][:, 64 * hh:64 * hh + 64],
                        self.recip_c[:, hh:hh + 1], None, ALU.mult),
                        wait=merge(trc, av_last)))
                av_free = merge(*hev)
                sums_z = ev("vector", lambda e: e.memset(self.sums_c[:], 0.0),
                            wait=av_free)
                heads_toks.extend(hev)
        self.tok_heads = merge(self.tok_heads, *heads_toks)

    def emit_wout(self, s):
        ev, ld = self.ev, self.ld
        fence = self.pr.fence_all()
        x1_toks = []
        hfw_busy = fence
        t_free = [fence, fence]
        ao_psum_free = [fence, fence]
        pb6_free = fence
        ao_ev_busy = [fence, fence]
        wo_busy = [fence] * 6
        xres_busy = [fence] * 4
        for w in range(4):
            hfw_toks = []
            for ti in range(4):
                tt = 4 * w + ti
                tts = {}
                for j in range(ND):
                    half = j // 4
                    bank = self.PB[4 + half]
                    tts[j] = ev("tensor", lambda e, tt=tt, j=j, b=bank:
                                e.transpose(
                        b[:, (j % 4) * P:(j % 4 + 1) * P],
                        self.heads_t[:, tt, j * P:(j + 1) * P],
                        self.ident_f[:]),
                        wait=merge(self.tok_heads, t_free[half], hfw_busy))
                for half in range(2):
                    bank = self.PB[4 + half]
                    tk = tts[half * 4 + 3]
                    dsth = self.hfw[:, 0, half * 4:half * 4 + 4,
                                    ti * P:(ti + 1) * P]
                    dstl = self.hfw[:, 1, half * 4:half * 4 + 4,
                                    ti * P:(ti + 1) * P]
                    srcb = bank[:, 0:512].rearrange("p (n c) -> p n c", n=4)
                    th = ev("vector", lambda e, d=dsth, sr=srcb: e.tensor_copy(
                        d, sr), wait=tk)
                    tl = ev("vector", lambda e, d=dstl, sr=srcb, hi=dsth:
                            e.tensor_tensor(d, sr, hi, ALU.subtract), wait=th)
                    t_free[half] = tl
                    hfw_toks.append(tl)
            tok_hfw = merge(*hfw_toks)
            xtoks = []
            if self.wlimit >= 2:
                for ti in range(4):
                    tt = 4 * w + ti
                    xtoks.append(self.ld(self.xres[:, ti, :],
                                         self.x_d[s, tt * P:(tt + 1) * P, :],
                                         wait=xres_busy[ti], dsem=f"dxr{ti}"))
            else:
                xtoks = [Tok()] * 4
            if self.wlimit < 3:
                continue
            new_hfw_busy = []
            for m in range(ND):
                pbank = self.PB[m % 2]
                mm_tok = None
                for k in range(ND):
                    ws = (m * ND + k) % 6
                    wts = [self.st(self.wo_stage[:, ws, pol_i, :],
                                   self.s_woutT[pol][k, m],
                                   wait=merge(self.wprep_tok, wo_busy[ws]),
                                   q="sync", dsem=f"dwo{ws}")
                           for pol_i, pol in enumerate(("hi", "lo"))]
                    wt = merge(*wts, tok_hfw)
                    if k == 0:
                        wt = merge(wt, ao_psum_free[m % 2])
                    passes = [(0, 0), (0, 1), (1, 0)]
                    for pi, (hp, wp) in enumerate(passes):
                        stf = (k == 0 and pi == 0)
                        spf = (k == ND - 1 and pi == 2)
                        mm_tok = ev("tensor", lambda e, k=k, hp=hp, wp=wp,
                                    ws=ws, pb=pbank, stf=stf, spf=spf: e.matmul(
                            pb[:, 0:512],
                            self.wo_stage[:, ws, wp, :],
                            self.hfw[:, hp, k, :],
                            start=stf, stop=spf, skip_group_check=True),
                            wait=wt)
                        wt = None
                    wo_busy[ws] = mm_tok
                new_hfw_busy.append(mm_tok)
                if self.wlimit < 4:
                    ao_psum_free[m % 2] = mm_tok
                    continue
                tae = ev("vector", lambda e, m=m, pb=pbank: e.tensor_copy(
                    self.ao_ev[:, m % 2, :], pb[:, 0:512]),
                    wait=merge(mm_tok, ao_ev_busy[m % 2]))
                ao_psum_free[m % 2] = tae
                if self.wlimit < 4.5:
                    ao_ev_busy[m % 2] = tae
                    continue
                ttrs = []
                for ti in range(4):
                    ttrs.append(ev("tensor", lambda e, m=m, ti=ti: e.transpose(
                        self.PB[6][:, ti * P:(ti + 1) * P],
                        self.ao_ev[:, m % 2, ti * P:(ti + 1) * P],
                        self.ident_f[:]), wait=merge(tae, pb6_free)))
                ttr_all = ttrs[-1]
                adds = []
                for ti in range(4):
                    tt = 4 * w + ti
                    tadd = ev("vector", lambda e, m=m, tt=tt, ti=ti:
                              e.tensor_add(
                        self.x1_t[:, tt, m * P:(m + 1) * P],
                        self.PB[6][:, ti * P:(ti + 1) * P],
                        self.xres[:, ti, :][:, m * P:(m + 1) * P]),
                        wait=merge(ttrs[ti] if self.wlimit == 7 else ttr_all,
                                   xtoks[ti]))
                    adds.append(tadd)
                    x1_toks.append(tadd)
                pb6_free = merge(*adds)
                ao_ev_busy[m % 2] = pb6_free
            hfw_busy = merge(*new_hfw_busy)
            xres_busy = [merge(*x1_toks[-4 * ND:])] * 4
        self.tok_x1 = merge(*x1_toks)
        if s == 0:
            d = self.dbg_out("x1_0", [P, NT, D])
            if d is not None:
                self.st(d, self.x1_t[:], wait=self.tok_x1, dsem="ddbg")
            d = self.dbg_out("heads0", [P, NT, D])
            if d is not None:
                self.st(d, self.heads_t[:], wait=self.tok_heads, dsem="ddbg")

    def emit_ln2(self, s):
        ev = self.ev
        eps = 1e-5
        toks = []
        for tt in range(NT):
            t_sum = ev("vector", lambda e, tt=tt: e.reduce_sum(
                self.sc_a[:, 0:1], self.x1_t[:, tt, :], axis=AX.X),
                wait=self.tok_x1)
            t_sq = ev("scalar", lambda e, tt=tt: e.activation(
                self.x1h_ring[:, 0, :], self.x1_t[:, tt, :], AF.Square,
                accum_out=self.sc_a[:, 1:2]), wait=self.tok_x1)
            tk = ev("vector", lambda e: e.tensor_scalar(
                self.sc_a[:, 2:3], self.sc_a[:, 0:1], 1.0 / D, None, ALU.mult),
                wait=merge(t_sum, t_sq))
            tk = ev("vector", lambda e: e.tensor_tensor(
                self.sc_a[:, 3:4], self.sc_a[:, 2:3], self.sc_a[:, 2:3],
                ALU.mult), wait=tk)
            tk = ev("vector", lambda e: e.scalar_tensor_tensor(
                self.sc_a[:, 4:5], self.sc_a[:, 1:2], 1.0 / D, self.sc_a[:, 3:4],
                ALU.mult, ALU.subtract), wait=tk)
            tk = ev("scalar", lambda e: e.activation(
                self.sc_a[:, 5:6], self.sc_a[:, 4:5], AF.Sqrt,
                bias=self.eps_c[:, 0:1], scale=1.0), wait=tk)
            tk = ev("vector", lambda e, tt=tt: e.reciprocal(
                self.ln2_r[:, tt:tt + 1], self.sc_a[:, 5:6]), wait=tk)
            toks.append(ev("vector", lambda e, tt=tt: e.scalar_tensor_tensor(
                self.ln2_m[:, tt:tt + 1], self.sc_a[:, 2:3], -1.0,
                self.ln2_r[:, tt:tt + 1], ALU.mult, ALU.mult), wait=tk))
        self.tok_ln2 = merge(*toks)

    def emit_ffn_w1(self, s, w):
        ev = self.ev
        fence = self.pr.fence_all()
        if w == 0:
            self.x1h_busy = [fence]
            self.w12_busy = [fence] * 4
            self.h_psum_free = [fence, fence]
            self.h_busy = [fence] * NF1
            self.pbb_free2 = fence
            self.pb6_free2 = fence
            self.y_psum_free = [fence, fence]
            self.ao2_busy = [fence]
            self.prev2_tok = [None] * NF1
            self.hq_busy = [fence] * NF1
            self.qf16_busy = fence
            self.xqb_busy = fence
            self.acc_busy = fence
            self.yw_busy = fence
            self.y_dma_last = Tok()
            self.pb3_free = fence
            self.pb2_free = fence
            self.qrow_tail_busy = fence
        xq_toks = []
        for ti in range(4):
            tt = 4 * w + ti
            xsl = 0
            t_xh = ev("scalar", lambda e, tt=tt, xsl=xsl: e.activation(
                self.x1h_ring[:, xsl, :], self.x1_t[:, tt, :], AF.Identity,
                bias=self.ln2_m[:, tt:tt + 1], scale=self.ln2_r[:, tt:tt + 1]),
                wait=merge(self.tok_ln2, self.x1h_busy[xsl]))
            t_am = ev("vector", lambda e, xsl=xsl: e.reduce_max(
                self.sc_a[:, 6:7], self.x1h_ring[:, xsl, :], axis=AX.X,
                apply_absolute_value=True), wait=t_xh)
            t_am = ev("vector", lambda e: e.tensor_scalar(
                self.sc_a[:, 7:8], self.sc_a[:, 6:7], 1e-5, None, ALU.max),
                wait=t_am)
            t_s = ev("vector", lambda e, tt=tt: e.tensor_scalar(
                self.rs1tok[:, tt:tt + 1], self.sc_a[:, 7:8], 1.0 / 127.0,
                None, ALU.mult), wait=t_am)
            t_s2 = ev("vector", lambda e, tt=tt: e.reciprocal(
                self.s1tok[:, tt:tt + 1], self.rs1tok[:, tt:tt + 1]),
                wait=t_s)
            tq1 = ev("vector", lambda e, tt=tt, xsl=xsl: e.tensor_scalar(
                self.qf16[:, 0:D], self.x1h_ring[:, xsl, :],
                self.s1tok[:, tt:tt + 1], MAGIC, ALU.mult, ALU.add),
                wait=merge(t_s2, self.qf16_busy))
            tq2 = ev("vector", lambda e: e.tensor_scalar(
                self.xqb_st[:, :], self.qf16[:, 0:D],
                -MAGIC, None, ALU.add), wait=merge(tq1, self.xqb_busy))
            self.qf16_busy = tq2
            self.x1h_busy[xsl] = tq1
            tts = []
            for j in range(ND):
                tts.append(ev("tensor", lambda e, j=j: e.transpose(
                    self.PBB[:, j * P:(j + 1) * P],
                    self.xqb_st[:, j * P:(j + 1) * P], self.ident_b[:]),
                    wait=merge(tq2, self.pbb_free2)))
            tc = ev("vector", lambda e, ti=ti: e.tensor_copy(
                self.xq_win[:, :, ti * P:(ti + 1) * P],
                self.PBB[:, 0:ND * P].rearrange("p (n c) -> p n c", n=ND)),
                wait=tts[-1])
            self.pbb_free2 = tc
            self.xqb_busy = tc
            xq_toks.append(tc)
        tok_xq = merge(*xq_toks)
        tt0 = 4 * w
        tmts = []
        for ti in range(4):
            tmts.append(ev("tensor", lambda e, tt0=tt0, ti=ti: e.transpose(
                self.PB[3][0:1, ti * P:(ti + 1) * P],
                self.rs1tok[:, tt0 + ti:tt0 + ti + 1], self.ident_f[:]),
                wait=merge(tok_xq, self.pb3_free)))
        bts = []
        rb = Tok()
        tcs = []
        for ti in range(4):
            tc = ev("vector", lambda e, ti=ti: e.tensor_copy(
                self.qrow[0:1, 0:P], self.PB[3][0:1, ti * P:(ti + 1) * P]),
                wait=merge(tmts[-1], rb))
            tcs.append(tc)
            mmb = ev("tensor", lambda e, ti=ti: e.matmul(
                self.PB[6][:, ti * P:(ti + 1) * P], self.ones_r[:],
                self.qrow[0:1, 0:P], start=True, stop=True),
                wait=merge(tc, self.pb6_free2 if ti == 0 else Tok()))
            rb = mmb
            bts.append(mmb)
        self.pb3_free = merge(*tcs)
        tok_bc = ev("vector", lambda e: e.tensor_scalar(
            self.bc_w1[:, 0:512], self.PB[6][:, 0:512],
            self.wsc[:, 1:2], None, ALU.mult), wait=merge(*bts))
        self.pb6_free2 = tok_bc
        h_toks = []
        for fc in range(NF1):
            wsl = fc % 2
            wt = self.st(
                self.w12_ring[:, wsl, :, :],
                self.s_w1qT[:, fc].transpose([1, 0, 2]),
                wait=merge(self.wprep_tok, self.w12_busy[wsl]), q="sync",
                dsem=f"dw12_{wsl}")
            pb = self.PB[fc % 2]
            mm = None
            for d in range(ND):
                mm = ev("tensor", lambda e, d=d, pb=pb, wsl=wsl: e.matmul(
                    pb[:, 0:512], self.w12_ring[:, wsl, d, :],
                    self.xq_win[:, d, :],
                    start=(d == 0), stop=(d == ND - 1),
                    skip_group_check=True),
                    wait=merge(wt, tok_xq,
                               self.h_psum_free[fc % 2] if d == 0 else None))
            self.w12_busy[wsl] = mm
            the = ev("vector", lambda e, fc=fc, pb=pb: e.tensor_tensor(
                self.h_t[:, fc, 2:514], pb[:, 0:512], self.bc_w1[:, 0:512],
                ALU.mult), wait=merge(mm, tok_bc, self.h_busy[fc]))
            self.h_psum_free[fc % 2] = the
            self.h_busy[fc] = the
            h_toks.append(the)
        self.tok_h = merge(*h_toks)
        if s == 1 and w == 3:
            d = self.dbg_out("hraw", [P, NF1, 512])
            if d is not None:
                tokd = self.st(d, self.h_t[:, :, 2:514], wait=self.tok_h,
                               dsem="ddbg")
                self.tok_h = merge(self.tok_h, tokd)
                self.h_busy = [merge(t, tokd) for t in self.h_busy]
            d2 = self.dbg_out("bcw1", [P, 512])
            if d2 is not None:
                tk2 = self.st(d2, self.bc_w1[:, 0:512], wait=self.tok_h,
                              dsem="ddbg")
                self.tok_h = merge(self.tok_h, tk2)
                self.t5_free = merge(getattr(self, "t5_free", Tok()), tk2)

    def emit_conv(self, s, w):
        ev = self.ev
        last = (w == 3)
        hs_toks = []
        acc_busy = self.acc_busy
        for fc in range(NF1):
            htile = self.h_t[:, fc, :]
            hb = self.h_busy[fc]
            if w == 0:
                tz = ev("vector", lambda e, ht=htile: e.memset(ht[:, 0:2], 0.0),
                        wait=hb)
            else:
                tz = ev("vector", lambda e, ht=htile, fc=fc: e.tensor_copy(
                    ht[:, 0:2], self.prev2[:, 2 * fc:2 * fc + 2]),
                    wait=merge(hb, self.prev2_tok[fc]))
            if last:
                tz = merge(tz, ev("vector", lambda e, ht=htile: e.memset(
                    ht[:, 514:515], 0.0), wait=hb))
            tp = ev("vector", lambda e, ht=htile, fc=fc: e.tensor_copy(
                self.prev2[:, 2 * fc:2 * fc + 2], ht[:, 512:514]), wait=tz)
            self.prev2_tok[fc] = tp
            ncen = 513 if last else 512
            w0 = self.convw_s[:, 3 * fc:3 * fc + 1]
            w1c = self.convw_s[:, 3 * fc + 1:3 * fc + 2]
            w2c = self.convw_s[:, 3 * fc + 2:3 * fc + 3]
            t1 = ev("vector", lambda e, ht=htile, n=ncen, w0=w0: e.tensor_scalar(
                self.acc_c[:, 0:n], ht[:, 0:n], w0, None, ALU.mult),
                wait=merge(tp, self.tok_consts, acc_busy, self.ao2_busy[0]))
            t2 = ev("vector", lambda e, ht=htile, n=ncen, w1c=w1c:
                    e.scalar_tensor_tensor(
                self.acc_c[:, 0:n], ht[:, 1:1 + n], w1c, self.acc_c[:, 0:n],
                ALU.mult, ALU.add), wait=t1)
            t3 = ev("vector", lambda e, ht=htile, n=ncen, w2c=w2c:
                    e.scalar_tensor_tensor(
                self.acc_c[:, 0:n], ht[:, 2:2 + n], w2c, self.acc_c[:, 0:n],
                ALU.mult, ALU.add), wait=t2)
            tsl = ev("scalar", lambda e, ht=htile, n=ncen, fc=fc: e.activation(
                ht[:, 1:1 + n], self.acc_c[:, 0:n], AF.Silu,
                bias=self.convb_s[:, fc:fc + 1], scale=1.0), wait=t3)
            acc_busy = tsl
            self.h_busy[fc] = tsl
            hs_toks.append(tsl)
        self.acc_busy = acc_busy
        self.tok_hs = merge(*hs_toks)

    def emit_quant2_w2(self, s, w):
        ev = self.ev
        col0 = 2 if w == 0 else 1
        ncols = (514 if w == 3 else 513) - col0
        tok0 = 0 if w == 0 else 512 * w - 1
        am = ev("vector", lambda e, c0=col0, n=ncols: e.tensor_copy(
            self.acc_c[:, 0:n], self.h_t[:, 0, c0:c0 + n]),
            wait=merge(self.tok_hs, self.acc_busy))
        am = ev("vector", lambda e, c0=col0, n=ncols: e.tensor_copy(
            self.acc2[:, 0:n], self.h_t[:, 0, c0:c0 + n]), wait=am)
        for fc in range(1, NF1):
            am = ev("vector", lambda e, fc=fc, c0=col0, n=ncols: e.tensor_tensor(
                self.acc_c[:, 0:n], self.acc_c[:, 0:n],
                self.h_t[:, fc, c0:c0 + n], ALU.max), wait=am)
            am = ev("vector", lambda e, fc=fc, c0=col0, n=ncols: e.tensor_tensor(
                self.acc2[:, 0:n], self.acc2[:, 0:n],
                self.h_t[:, fc, c0:c0 + n], ALU.min), wait=am)
        am = ev("vector", lambda e, n=ncols: e.scalar_tensor_tensor(
            self.acc_c[:, 0:n], self.acc2[:, 0:n], -1.0, self.acc_c[:, 0:n],
            ALU.mult, ALU.max), wait=am)
        nti = (ncols + P - 1) // P
        s2_toks = []
        zz = ev("vector", lambda e: e.memset(self.s2tok[:], 1.0))
        zz = ev("vector", lambda e: e.memset(self.rs2tok[:], 1.0), wait=zz)
        prev_tr = zz
        for ti in range(nti):
            cw = min(P, ncols - ti * P)
            tt1 = ev("tensor", lambda e, ti=ti, cw=cw: e.transpose(
                self.PB[6][0:cw, 0:P],
                self.acc_c[:, ti * P:ti * P + cw],
                self.ident_f[:]), wait=merge(am, self.pb6_free2, prev_tr))
            tr1 = ev("vector", lambda e, ti=ti, cw=cw: e.reduce_max(
                self.s2tok[0:cw, ti:ti + 1],
                self.PB[6][0:cw, 0:P], axis=AX.X), wait=merge(tt1, zz))
            tr1 = ev("vector", lambda e, ti=ti: e.tensor_scalar(
                self.s2tok[:, ti:ti + 1], self.s2tok[:, ti:ti + 1], 1e-5,
                None, ALU.max), wait=tr1)
            tr1 = ev("vector", lambda e, ti=ti: e.tensor_scalar(
                self.rs2tok[:, ti:ti + 1], self.s2tok[:, ti:ti + 1],
                1.0 / 127.0, None, ALU.mult), wait=tr1)
            s2_toks.append(ev("vector", lambda e, ti=ti: e.reciprocal(
                self.s2tok[:, ti:ti + 1], self.rs2tok[:, ti:ti + 1]),
                wait=tr1))
            prev_tr = s2_toks[-1]
        tok_s2 = merge(*s2_toks)
        self.pb6_free2 = tok_s2
        if self.qlimit < 2:
            return
        tm_s, tm_r = [], []
        for ti in range(min(nti, 4)):
            tm_s.append(ev("tensor", lambda e, ti=ti: e.transpose(
                self.PB[3][0:1, ti * P:(ti + 1) * P],
                self.s2tok[:, ti:ti + 1], self.ident_f[:]),
                wait=merge(tok_s2, self.pb3_free)))
            tm_r.append(ev("tensor", lambda e, ti=ti: e.transpose(
                self.PB[2][0:1, ti * P:(ti + 1) * P],
                self.rs2tok[:, ti:ti + 1], self.ident_f[:]),
                wait=merge(tok_s2, self.pb2_free)))
        tmt2 = merge(*tm_s, *tm_r)
        rb = Tok()
        bq, br = [], []
        tcs2 = []
        for ti in range(min(nti, 4)):
            cw = min(P, ncols - ti * P)
            for which in range(2):
                srccol = 0 if which == 0 else 128
                pbsrc = self.PB[3] if which == 0 else self.PB[2]
                tc = ev("vector", lambda e, ti=ti, pbs=pbsrc: e.tensor_copy(
                    self.qrow[0:1, 0:P], pbs[0:1, ti * P:(ti + 1) * P]),
                    wait=merge(tmt2, rb))
                tcs2.append(tc)
                dst = self.PB[5] if which == 0 else self.PB[4]
                mmb = ev("tensor", lambda e, ti=ti, cw=cw, dst=dst: e.matmul(
                    dst[:, ti * P:ti * P + cw], self.ones_r[:],
                    self.qrow[0:1, 0:cw], start=True, stop=True),
                    wait=merge(tc, (self.t5_free if hasattr(self, 't5_free')
                                    else Tok()) if ti == 0 else Tok(),
                               (self.t4_free if hasattr(self, 't4_free')
                                else Tok()) if ti == 0 else Tok()))
                rb = mmb
                (bq if which == 0 else br).append(mmb)
        n4 = min(ncols, 512)
        tbq = ev("vector", lambda e, n=n4: e.tensor_copy(
            self.bc_q[:, 0:n], self.PB[5][:, 0:n]), wait=merge(*bq))
        tbr = ev("vector", lambda e, n=n4: e.tensor_scalar(
            self.bc_r[:, 0:n], self.PB[4][:, 0:n], self.wsc[:, 3:4], None,
            ALU.mult), wait=merge(*br))
        if nti == 5:
            for which in range(2):
                pbsrc = self.PB[3] if which == 0 else self.PB[2]
                srct = self.s2tok if which == 0 else self.rs2tok
                tmx = ev("tensor", lambda e, pbs=pbsrc, st_=srct: e.transpose(
                    pbs[0:1, 0:P], st_[:, 4:5], self.ident_f[:]),
                    wait=merge(tok_s2, rb))
                tc = ev("vector", lambda e, pbs=pbsrc: e.tensor_copy(
                    self.qrow[0:1, 0:P], pbs[0:1, 0:P]), wait=merge(tmx, rb))
                tcs2.append(tc)
                dst = self.PB[5] if which == 0 else self.PB[4]
                mmb = ev("tensor", lambda e, dst=dst: e.matmul(
                    dst[:, 0:1], self.ones_r[:], self.qrow[0:1, 0:1],
                    start=True, stop=True),
                    wait=merge(tc, tbq if which == 0 else tbr))
                rb = mmb
                if which == 0:
                    tbq = merge(tbq, ev("vector", lambda e: e.tensor_copy(
                        self.bc_q[:, 512:513], self.PB[5][:, 0:1]), wait=mmb))
                else:
                    tbr = merge(tbr, ev("vector", lambda e: e.tensor_scalar(
                        self.bc_r[:, 512:513], self.PB[4][:, 0:1],
                        self.wsc[:, 3:4], None, ALU.mult), wait=mmb))
        self.t5_free = tbq
        self.t4_free = tbr
        self.pb3_free = merge(self.pb3_free, *tcs2)
        self.pb2_free = merge(self.pb2_free, *tcs2)
        if self.qlimit < 3:
            return
        hq_toks = []
        qb = merge(tbq)
        for fc in range(NF1):
            tq1 = ev("vector", lambda e, fc=fc, c0=col0, n=ncols:
                     e.tensor_tensor(
                self.acc_c[:, 0:n], self.h_t[:, fc, c0:c0 + n],
                self.bc_q[:, 0:n], ALU.mult),
                wait=merge(qb, self.h_busy[fc]))
            tq2 = ev("vector", lambda e, n=ncols: e.tensor_scalar(
                self.qf16[:, 0:n], self.acc_c[:, 0:n], MAGIC, None, ALU.add),
                wait=merge(tq1, self.qf16_busy))
            tq3 = ev("vector", lambda e, fc=fc, n=ncols: e.tensor_scalar(
                self.hq_t[:, fc, 0:n], self.qf16[:, 0:n],
                -MAGIC, None, ALU.add), wait=merge(tq2, self.hq_busy[fc]))
            self.qf16_busy = tq3
            self.hq_busy[fc] = tq3
            self.h_busy[fc] = tq1
            qb = tq1
            hq_toks.append(tq3)
        self.acc_busy = merge(*hq_toks[-1:])
        tok_hq = merge(*hq_toks)
        if self.qlimit < 4:
            return
        n512 = min(ncols, 512)
        tail = ncols - n512
        for m in range(ND):
            pb = self.PB[m % 2]
            mm = None
            wt = None
            for fb in range(4):
                wsl = 2 + (fb % 2)
                wt = self.st(
                    self.w12_ring[:, wsl, :, :],
                    self.s_w2qT[fb * 8:fb * 8 + 8, m].transpose([1, 0, 2]),
                    wait=merge(self.wprep_tok, self.w12_busy[wsl]),
                    q="sync", dsem=f"dw12_{wsl}")
                for j in range(8):
                    fc = fb * 8 + j
                    mm = ev("tensor", lambda e, fc=fc, j=j, pb=pb, wsl=wsl,
                            n=n512: e.matmul(
                        pb[:, 0:n], self.w12_ring[:, wsl, j, :],
                        self.hq_t[:, fc, 0:n],
                        start=(fc == 0), stop=(fc == NF1 - 1 and True),
                        skip_group_check=True),
                        wait=merge(wt, tok_hq,
                                   self.y_psum_free[m % 2] if fc == 0 else None))
                    if tail:
                        mm = ev("tensor", lambda e, fc=fc, j=j, wsl=wsl: e.matmul(
                            self.PB[2][:, 0:1], self.w12_ring[:, wsl, j, :],
                            self.hq_t[:, fc, 512:513],
                            start=(fc == 0), stop=(fc == NF1 - 1),
                            skip_group_check=True),
                            wait=merge(self.tail_free, self.pb2_free)
                            if fc == 0 else None)
                self.w12_busy[wsl] = mm
            tye = ev("vector", lambda e, pb=pb, n=n512: e.tensor_tensor(
                self.acc_c[:, 0:n], pb[:, 0:n], self.bc_r[:, 0:n],
                ALU.mult), wait=merge(mm, tbr, self.ao2_busy[0]))
            self.y_psum_free[m % 2] = tye
            if tail:
                tyet = ev("vector", lambda e, m=m: e.scalar_tensor_tensor(
                    self.sc_a[:, 8 + m:9 + m], self.PB[2][:, 0:1], 1.0,
                    self.bc_r[:, 512:513], ALU.mult, ALU.mult), wait=merge(mm, tbr))
                self.tail_free = tyet
            if self.qlimit < 5:
                continue
            tcp = None
            if tail:
                tcp = ev("vector", lambda e, m=m: e.tensor_copy(
                    self.acc_c[:, 512:513], self.sc_a[:, 8 + m:9 + m]),
                    wait=tyet)
            j0b = 0 if w == 0 else 1
            tts = []
            for ti in range(4):
                j0 = j0b + ti * P
                cw = min(P, ncols - j0)
                tts.append(ev("tensor", lambda e, ti=ti, j0=j0, cw=cw:
                              e.transpose(
                    self.PB[6][0:cw, ti * P:ti * P + P],
                    self.acc_c[:, j0:j0 + cw], self.ident_f[:]),
                    wait=merge(tye, tcp, self.pb6_free2)))
            tts_all = tts[-1]
            if self.qlimit < 6:
                self.pb6_free2 = tts_all
                self.ao2_busy[0] = tts_all
                continue
            adds = []
            for ti in range(4):
                j0 = j0b + ti * P
                cw = min(P, ncols - j0)
                tadd = ev("vector", lambda e, ti=ti, cw=cw, m=m, w=w:
                          e.tensor_add(
                    self.x1_t[0:cw, 4 * w + ti, m * P:(m + 1) * P],
                    self.PB[6][0:cw, ti * P:(ti + 1) * P],
                    self.x1_t[0:cw, 4 * w + ti, m * P:(m + 1) * P]),
                    wait=merge(tts_all, self.tok_x1))
                adds.append(tadd)
                self.out_ready[4 * w + ti] = merge(
                    self.out_ready.get(4 * w + ti), tadd)
            self.pb6_free2 = merge(*adds)
            self.ao2_busy[0] = tts_all
            if w >= 1:
                ttl = ev("tensor", lambda e: e.transpose(
                    self.PB[3][0:1, 0:P], self.acc_c[:, 0:1],
                    self.ident_f[:]), wait=merge(tye, self.pb3_free))
                tev_t = ev("vector", lambda e: e.tensor_copy(
                    self.qrow[64:65, 0:P], self.PB[3][0:1, 0:P]),
                    wait=merge(ttl, self.qrow_tail_busy))
                self.pb3_free = tev_t
                ldma = self.st(
                    self.x1_t[127:128, 4 * w - 1, m * P:(m + 1) * P],
                    self.qrow[64:65, 0:P], q="gpsimd", dsem=f"dyadd{m % 2}",
                    accum=True, wait=merge(tev_t, self.tok_x1))
                self.qrow_tail_busy = ldma
                self.out_ready[4 * w - 1] = merge(
                    self.out_ready.get(4 * w - 1), ldma)
                self.ao2_busy[0] = merge(self.ao2_busy[0], ttl)
                self.y_dma_last = merge(self.y_dma_last, ldma)

    def build(self):
        self.declare()
        es = contextlib.ExitStack()
        with es:
            self.pr = Prog(self.nc, es)
            for name in ("d_sp", "d_gp", "p_pe", "p_dve", "p_act", "p_gp"):
                self.pr.sem(name)
            block = es.enter_context(self.nc.Block())

            self.emit_setup()
            out_toks = []
            L = self.limit
            for s in range(BLOC):
                self.tok_heads = Tok()
                self.out_ready = {}
                self.tail_free = Tok()
                if L >= 1:
                    self.emit_ln1(s)
                for sg in range(NSG):
                    if L >= 2:
                        self.emit_qkv(s, sg)
                    if L >= 3:
                        self.emit_attention(s, sg)
                if L >= 4:
                    self.emit_wout(s)
                if L >= 5:
                    self.emit_ln2(s)
                for w in range(4):
                    if L >= 6:
                        self.emit_ffn_w1(s, w)
                    if L >= 7:
                        self.emit_conv(s, w)
                    if L >= 8:
                        self.emit_quant2_w2(s, w)
                if L < 3.5:
                    continue
                for tt in range(NT):
                    out_toks.append(self.st(
                        self.out_d[s, tt * P:(tt + 1) * P, :],
                        self.x1_t[:, tt, :],
                        wait=self.out_ready.get(tt, Tok()),
                        dsem=f"dout{tt % 4}"))
            total = merge(*out_toks, getattr(self, "y_dma_last", None))
            self.pr.emit("sync", lambda e: e.nop(), wait=total)
            self.pr.emit("gpsimd", lambda e: e.engine_nop(), wait=total)
            self.pr.run(block)
        return self.nc


def kernel(**inputs):
    x = np.ascontiguousarray(inputs["L0_attn_hidden_states"])
    wqkv = np.ascontiguousarray(inputs["L0_attn_w_qkv"])
    wout = np.ascontiguousarray(inputs["L0_attn_w_out"])
    w1 = np.ascontiguousarray(inputs["L0_ffn_w1"])
    convw = np.ascontiguousarray(
        np.asarray(inputs["L0_ffn_conv_w"]).reshape(NF1, P, 3)
        .transpose(1, 0, 2).reshape(P, NF1 * 3))
    convb = np.ascontiguousarray(
        np.asarray(inputs["L0_ffn_conv_b"]).reshape(NF1, P)
        .transpose(1, 0))
    w2 = np.ascontiguousarray(inputs["L0_ffn_w2"])

    mk = MK()
    nc = mk.build()
    in_maps = [{
        "x": x[c * BLOC:(c + 1) * BLOC],
        "wqkv": wqkv, "wout": wout, "w1": w1,
        "convw": convw, "convb": convb, "w2": w2,
    } for c in range(NCORES)]
    res = run_bass_kernel_spmd(nc, in_maps, core_ids=list(range(NCORES)))
    return np.concatenate([r["out"] for r in res.results], axis=0)
